# version 12
# baseline (speedup 1.0000x reference)
import math
from contextlib import ExitStack

import numpy as np
import ml_dtypes

import concourse.bass as bass
import concourse.mybir as mybir
from concourse.bass_utils import run_bass_kernel_spmd

F32 = mybir.dt.float32
BF16 = mybir.dt.bfloat16
AF = mybir.ActivationFunctionType
ALU = mybir.AluOpType
AX = mybir.AxisListType
NPBF = ml_dtypes.bfloat16

B, T, D = 2, 8192, 1024
TL = 2048
H, DK, DV = 4, 256, 512
CP = 256
NSC = TL // CP
NPFX = (T - TL) // CP
PFX_TOK = T - TL
NH, HD = 16, 64
E, FF = 8, 2048
EPS = 1e-5
GAMMA = [1.0 - 2.0 ** (-5.0 - h) for h in range(H)]
GC = [float(np.float32(g) ** CP) for g in GAMMA]

PHASE_LIMIT = 4
DBG = False


class Sched:
    ENGS = ("tensor", "vector", "scalar", "gpsimd", "sync")

    def __init__(self):
        self.q = {e: [] for e in self.ENGS}
        self.cnt = {}
        self.last_wait = {}

    def _reg(self, sem):
        self.cnt.setdefault(sem, 0)

    def emit(self, eng, fn, inc=None, amt=1, after=()):
        seen = {}
        for tok in after:
            if tok is None:
                continue
            sem, val = tok
            if val > seen.get(sem, 0):
                seen[sem] = val
        for sem, val in seen.items():
            self._reg(sem)
            if self.last_wait.get((eng, sem), 0) >= val:
                continue
            self.last_wait[(eng, sem)] = val
            self.q[eng].append(("w", sem, val))
        tok = None
        if inc is not None:
            self._reg(inc)
            self.cnt[inc] += amt
            tok = (inc, self.cnt[inc])
        self.q[eng].append(("o", fn, [] if inc is None else [(inc, amt)]))
        return tok

    def tok(self, sem):
        self._reg(sem)
        return (sem, self.cnt[sem])

    def barrier(self):
        toks = [self.tok(s) for s in list(self.cnt)]
        for e in self.ENGS:
            for t_ in toks:
                self.emit(e, lambda E: E.nop(), after=[t_])

    def replay(self, block, sems):
        def mk(name):
            def run(Eng):
                for item in self.q[name]:
                    if item[0] == "w":
                        Eng.wait_ge(sems[item[1]], item[2])
                    else:
                        ins = item[1](Eng)
                        for sem, amt in item[2]:
                            ins.then_inc(sems[sem], amt)
            return run
        block.tensor(mk("tensor"))
        block.vector(mk("vector"))
        block.scalar(mk("scalar"))
        block.gpsimd(mk("gpsimd"))
        block.sync(mk("sync"))


class Buf:
    __slots__ = ("ap", "wr", "rd", "name")

    def __init__(self, ap, name=""):
        self.ap = ap
        self.wr = None
        self.rd = []
        self.name = name

    def __getitem__(self, key):
        return self.ap[key]


class Ctx:

    def __init__(self, S):
        self.S = S

    def op(self, eng, fn, ins=(), outs=(), inc=None, amt=1, extra=()):
        after = list(extra)
        for b in ins:
            after.append(b.wr)
        for b in outs:
            after.append(b.wr)
            after.extend(b.rd)
        tok = self.S.emit(eng, fn, inc=inc, amt=amt, after=after)
        for b in ins:
            if tok is not None:
                b.rd.append(tok)
        for b in outs:
            b.wr = tok
            b.rd = []
        return tok

    def mm(self, ps, lhsT, rhs, start, stop, psbuf, ins=(), inc=None):
        return self.op("tensor",
                       lambda E, p=ps, l=lhsT, r=rhs, a=start, b=stop:
                       E.matmul(p, l, r, start=a, stop=b),
                       ins=ins, outs=(psbuf,) if stop else (),
                       inc=inc if stop else None)

    def mm_group(self, psbuf, ps_ap, pairs, ins=(), inc="pe"):
        n = len(pairs)
        after = []
        for b in ins:
            after.append(b.wr)
        after.append(psbuf.wr)
        after.extend(psbuf.rd)
        tok = None
        for i, (l, r) in enumerate(pairs):
            tok = self.S.emit("tensor",
                              lambda E, p=ps_ap, L=l, R=r, a=(i == 0), b=(i == n - 1):
                              E.matmul(p, L, R, start=a, stop=b),
                              inc=inc if i == n - 1 else None,
                              after=after if i == 0 else ())
        for b in ins:
            b.rd.append(tok)
        psbuf.wr = tok
        psbuf.rd = []
        return tok

    def transp(self, psbuf, ps_ap, in_ap, id_ap, ins=(), inc="pe", last=True,
               first=True):
        after = []
        for b in ins:
            after.append(b.wr)
        if first:
            after.append(psbuf.wr)
            after.extend(psbuf.rd)
        tok = self.S.emit("tensor",
                          lambda E, p=ps_ap, i=in_ap, d=id_ap: E.transpose(p, i, d),
                          inc=inc if last else None, after=after if first else ())
        if last:
            for b in ins:
                b.rd.append(tok)
            psbuf.wr = tok
            psbuf.rd = []
        return tok

    def TT(self, outb, o, a_b, a, b_b, b_ap, alu=ALU.mult, inc="ve", eng="vector"):
        return self.op(eng, lambda E, O=o, X=a, Y=b_ap, P=alu:
                       E.tensor_tensor(O, X, Y, P),
                       ins=(a_b, b_b), outs=(outb,), inc=inc)

    def TS(self, outb, o, inb, i, s1b, s1, s2b=None, s2=None, op0=ALU.mult,
           op1=ALU.mult, accb=None, acc=None, inc="ve", eng="vector"):
        assert not (acc is not None and s2 is not None), \
            "tensor_scalar: 2-op + accum_out drops op1 on HW"
        if s2 is None:
            op1 = ALU.bypass
        ins = [inb, s1b] + ([s2b] if s2b is not None else [])
        outs = [outb] + ([accb] if accb is not None else [])
        return self.op(eng, lambda E, O=o, I=i, A=s1, Bs=s2, p=op0, q=op1, ac=acc:
                       E.tensor_scalar(O, I, A, Bs, p, q, accum_out=ac),
                       ins=ins, outs=outs, inc=inc)

    def TSi(self, outb, o, inb, i, s1, s2=None, op0=ALU.mult, op1=ALU.mult,
            accb=None, acc=None, inc="ve", eng="vector"):
        if s2 is None:
            op1 = ALU.bypass
        outs = [outb] + ([accb] if accb is not None else [])
        return self.op(eng, lambda E, O=o, I=i, A=s1, Bs=s2, p=op0, q=op1, ac=acc:
                       E.tensor_scalar(O, I, A, Bs, p, q, accum_out=ac),
                       ins=(inb,), outs=outs, inc=inc)

    def STT(self, outb, o, i0b, i0, s, i1b, i1, op0, op1, accb=None, acc=None,
            sb_=None, inc="ve", eng="vector"):
        ins = [i0b, i1b] + ([sb_] if sb_ is not None else [])
        outs = [outb] + ([accb] if accb is not None else [])
        return self.op(eng, lambda E, O=o, A=i0, c=s, Bt=i1, p=op0, q=op1, ac=acc:
                       E.scalar_tensor_tensor(O, A, c, Bt, p, q, accum_out=ac),
                       ins=ins, outs=outs, inc=inc)

    def ACT(self, outb, o, inb, i, f, biasb=None, bias=0.0, scale=1.0,
            accb=None, acc=None, inc="ac"):
        ins = [inb] + ([biasb] if biasb is not None else [])
        outs = [outb] + ([accb] if accb is not None else [])
        return self.op("scalar", lambda E, O=o, I=i, F_=f, b=bias, s=scale, ac=acc:
                       E.activation(O, I, F_, bias=b, scale=s, accum_out=ac),
                       ins=ins, outs=outs, inc=inc)

    def CPY(self, outb, o, inb, i, inc="ve", eng="vector"):
        return self.op(eng, lambda E, O=o, I=i: E.tensor_copy(O, I),
                       ins=(inb,), outs=(outb,), inc=inc)

    def RED(self, outb, o, inb, i, alu=ALU.max, negate=False, inc="ve"):
        return self.op("vector", lambda E, O=o, I=i, P=alu, n=negate:
                       E.tensor_reduce(O, I, AX.X, P, negate=n),
                       ins=(inb,), outs=(outb,), inc=inc)

    def RECIP(self, outb, o, inb, i, inc="ve"):
        return self.op("vector", lambda E, O=o, I=i: E.reciprocal(O, I),
                       ins=(inb,), outs=(outb,), inc=inc)

    def load(self, outb, o, in_ap, q="sync", srcb=None):
        ins = (srcb,) if srcb is not None else ()
        return self.op(q, lambda E, O=o, I=in_ap: E.dma_start(out=O, in_=I),
                       ins=ins, outs=(outb,), inc="d0" if q == "sync" else "d1",
                       amt=16)

    def stor(self, out_ap, inb, i, q="sync", dstb=None, accum=False):
        outs = (dstb,) if dstb is not None else ()
        if accum:
            fn = lambda E, O=out_ap, I=i: E.dma_start(out=O, in_=I, accum_op=ALU.add)
        else:
            fn = lambda E, O=out_ap, I=i: E.dma_start(out=O, in_=I)
        return self.op(q, fn, ins=(inb,), outs=outs,
                       inc="d0" if q == "sync" else "d1", amt=16)

    def memset(self, outb, o, val, eng="gpsimd", inc="po"):
        return self.op(eng, lambda E, O=o, v=val: E.memset(O, v),
                       outs=(outb,), inc=inc)


class Arena:

    def __init__(self, t, n):
        self.t, self.n = t, n
        self.free = [(0, n, [])]
        self.live = {}

    def alloc(self, name, cols, dtype=BF16):
        need = cols * (2 if dtype == F32 else 1)
        need = (need + 1) // 2 * 2
        for i, (a, sz, toks) in enumerate(self.free):
            if sz >= need:
                if sz == need:
                    del self.free[i]
                else:
                    self.free[i] = (a + need, sz - need, list(toks))
                for nm2, (a2, s2, _) in self.live.items():
                    assert a + need <= a2 or a2 + s2 <= a, \
                        f"arena overlap: {name} vs {nm2}"
                self.live[name] = (a, need, None)
                ap = self.t[:, a:a + need]
                b = Buf(ap.bitcast(F32) if dtype == F32 else ap, name)
                b.rd = [t_ for t_ in toks if t_ is not None]
                self.live[name] = (a, need, b)
                return b
        raise MemoryError(f"arena OOM: {name} needs {need} free={self.free} "
                          f"live={sorted(self.live)}")

    def freebuf(self, *names):
        for name in names:
            a, sz, b = self.live.pop(name)
            toks = list(b.rd)
            if b.wr is not None:
                toks.append(b.wr)
            self.free.append((a, sz, toks))
        self.free.sort()
        merged = []
        for a2, s2, t2 in self.free:
            if merged and merged[-1][0] + merged[-1][1] == a2:
                merged[-1][1] += s2
                merged[-1][2] = merged[-1][2] + t2
            else:
                merged.append([a2, s2, list(t2)])
        self.free = [(m[0], m[1], m[2]) for m in merged]


def rearr_w(w_ap):
    return w_ap.rearrange("(c p) j -> p c j", p=128)


def build_nc(phase_limit=PHASE_LIMIT, dbg=DBG):
    nc = bass.Bass("TRN2", target_bir_lowering=False)
    S = Sched()
    C = Ctx(S)
    es = ExitStack()

    def din(name, shape, dtype=F32):
        return nc.dram_tensor(name, list(shape), dtype, kind="ExternalInput").ap()

    def dscr(name, shape, dtype=F32):
        kind = "ExternalOutput" if dbg else "Internal"
        return Buf(nc.dram_tensor(name, list(shape), dtype, kind=kind).ap(), name)

    IN = {}
    for nm, shp, dt in [
        ("xb", (T, D), F32), ("x_own", (TL, D), F32),
        ("Wq", (D, D), F32), ("Wk", (D, D), F32), ("Wv", (D, 2 * D), F32),
        ("Wg", (D, 2 * D), F32), ("Wo", (2 * D, D), F32),
        ("wiT", (D, 3 * D), F32), ("woT", (D, D), F32),
        ("w1", (E, D, FF), F32), ("w2", (E, FF, D), F32),
        ("wgrp", (D, 4), F32), ("wexp", (D, E), F32),
        ("ln_w", (1, D), F32), ("ln_b", (1, D), F32), ("gnw", (1, DV), F32),
        ("bi", (1, 3 * D), F32), ("bo_v", (1, D), F32),
        ("l1g", (1, D), F32), ("l1b", (1, D), F32),
        ("l2g", (1, D), F32), ("l2b", (1, D), F32),
        ("bgrp", (4, 1), F32), ("bexp", (E, 1), F32),
        ("b1", (E, FF), F32), ("b2", (E, D), F32),
        ("cos_tok", (T, 128), BF16), ("sin_tok", (T, 128), BF16),
        ("cos_own", (TL, 128), BF16), ("sin_own", (TL, 128), BF16),
        ("cos_f", (128, TL), BF16), ("sin_f", (128, TL), BF16),
        ("dmaskT", (H * 2 * 128, CP), F32), ("crossrep", (128, H * CP), BF16),
        ("kdec_c", (128, H * 2), F32), ("pfx_c", (128, NPFX * H * 2), F32),
        ("ident_b", (128, 128), BF16), ("ident_f", (128, 128), F32),
        ("ones_row_b", (1, 512), BF16), ("ones_row_f", (1, 128), F32),
        ("ones_col_b", (128, 1), BF16), ("pairsel", (128, 2), BF16),
        ("selrow8", (8, 8 * 128), BF16), ("zeros_col", (128, 1), F32),
    ]:
        IN[nm] = din(nm, shp, dt)

    out_d = nc.dram_tensor("out", [TL, D], F32, kind="ExternalOutput").ap()
    out_b = Buf(out_d, "out")
    krtok_d = dscr("krtok_d", (PFX_TOK, D), BF16)
    vrtok_d = dscr("vrtok_d", (PFX_TOK, 2 * D), BF16)
    mkT_d = dscr("mkT_d", (D, T), BF16)
    mv_d = dscr("mv_d", (T, D), BF16)
    qrT_d = dscr("qrT_d", (D, TL), BF16)
    gT_d = dscr("gT_d", (2 * D, TL), BF16)
    krown_d = dscr("krown_d", (TL, D), BF16)
    vrown_d = dscr("vrown_d", (TL, 2 * D), BF16)
    qmT_d = dscr("qmT_d", (D, TL), BF16)
    ow_d = dscr("ow_d", (2 * D, TL), BF16)
    retout_d = dscr("retout_d", (TL, D), F32)
    y_d = dscr("y_d", (TL, D), F32)
    h1_d = dscr("h1_d", (TL, D), F32)
    row_d = dscr("row_d", (4, 4096), F32)
    dbgd = {}
    if dbg:
        for nm in ["dbg_a", "dbg_b", "dbg_c"]:
            dbgd[nm] = Buf(nc.dram_tensor(nm, [128, 4096], F32,
                                          kind="ExternalOutput").ap(), nm)

    def sbuf(name, shape, dtype):
        h = es.enter_context(nc.sbuf_tensor(name, list(shape), dtype))
        return Buf(h[:, :], name)

    NAR = 79872
    ar_h = es.enter_context(nc.sbuf_tensor("arena", [128, NAR], BF16))
    A = Arena(ar_h[:, :], NAR)

    idb = sbuf("idb", (128, 128), BF16)
    idf = sbuf("idf", (128, 128), F32)
    ones_b = sbuf("ones_b", (1, 512), BF16)
    ones_f = sbuf("ones_f", (1, 128), F32)
    onescol = sbuf("onescol", (128, 1), BF16)
    zcol = sbuf("zcol", (128, 1), F32)
    psel = sbuf("psel", (128, 2), BF16)
    kdec_sb = sbuf("kdec_sb", (128, H * 2), F32)
    pfx_sb = sbuf("pfx_sb", (128, NPFX * H * 2), F32)
    gnw_sb = sbuf("gnw_sb", (128, 4), F32)
    lnw_sb = sbuf("lnw_sb", (128, 8), F32)
    lnbc = sbuf("lnbc", (128, 8), BF16)
    lnbc_f = sbuf("lnbc_f", (128, 8), F32)
    stc = sbuf("stc", (128, 40), F32)
    mu1o = sbuf("mu1o", (128, 16), F32)
    rs1o = sbuf("rs1o", (128, 16), F32)
    maxk_sb = sbuf("maxk_sb", (2, 128), F32)
    maxq_sb = sbuf("maxq_sb", (2, 32), F32)
    mh_bias = sbuf("mh_bias", (128, NH + 8), F32)
    brq = sbuf("brq", (128, 8), F32)
    brg = sbuf("brg", (128, 16), F32)
    bmq = sbuf("bmq", (128, 8), F32)
    bmk = sbuf("bmk", (128, 8), F32)
    brk_rep = sbuf("brk_rep", (128, D), BF16)
    brv_rep = sbuf("brv_rep", (128, 2 * D), BF16)
    bmv_rep = sbuf("bmv_rep", (128, D), BF16)
    vec8 = sbuf("vec8", (16, 128), F32)
    row3 = sbuf("row3", (1, TL), F32)
    drow = sbuf("drow", (1, TL), F32)
    dstage = sbuf("dstage", (1, TL), F32)
    dstage2 = sbuf("dstage2", (1, TL), F32)
    b1c = sbuf("b1c", (128, E * 16), F32)
    b2_sb = sbuf("b2_sb", (8, D), BF16)
    bgrp_sb = sbuf("bgrp_sb", (4, 1), F32)
    bexp_sb = sbuf("bexp_sb", (8, 1), F32)
    sel8_sb = sbuf("sel8_sb", (8, 8 * 128), BF16)
    wgrp_sb = sbuf("wgrp_sb", (128, 8 * 4), BF16)
    wexp_sb = sbuf("wexp_sb", (128, 8 * 8), BF16)

    def psum(name, shape, dtype=F32):
        h = es.enter_context(nc.psum_tensor(name, list(shape), dtype))
        return h[:, :]

    PSA = psum("PSA", (128, 2048))
    PSB = psum("PSB", (128, 1024))
    PSC = psum("PSC", (128, 512))
    PST = psum("PST", (128, 1024), BF16)
    psA = [Buf(PSA[:, i * 512:(i + 1) * 512], f"psA{i}") for i in range(4)]
    psAa = Buf(PSA[:, 0:1024], "psAa")
    psAb = Buf(PSA[:, 1024:2048], "psAb")
    psAw = Buf(PSA[:, :], "psAw")
    psB = [Buf(PSB[:, 0:512], "psB0"), Buf(PSB[:, 512:1024], "psB1")]
    psBw = Buf(PSB[:, :], "psBw")
    psC = Buf(PSC[:, :], "psC")
    psC2 = Buf(PSC[64:66, 0:512], "psC2")
    psT = Buf(PST[:, :], "psT")

    def alias(*bufs):
        deps = []
        for b in bufs:
            deps.append(b.wr)
            deps.extend(b.rd)
        return [d for d in deps if d is not None]

    sems = {}
    for s_ in ["d0", "d1", "pe", "ve", "ac", "po"]:
        sems[s_] = es.enter_context(nc.semaphore(s_))
        S._reg(s_)

    C.load(idb, idb.ap, IN["ident_b"])
    C.load(idf, idf.ap, IN["ident_f"])
    C.load(ones_b, ones_b.ap, IN["ones_row_b"])
    C.load(ones_f, ones_f.ap, IN["ones_row_f"])
    C.load(onescol, onescol.ap, IN["ones_col_b"])
    C.load(zcol, zcol.ap, IN["zeros_col"])
    C.load(psel, psel.ap, IN["pairsel"])
    C.load(kdec_sb, kdec_sb.ap, IN["kdec_c"])
    C.load(pfx_sb, pfx_sb.ap, IN["pfx_c"])
    C.load(sel8_sb, sel8_sb.ap, IN["selrow8"])
    C.load(bgrp_sb, bgrp_sb.ap, IN["bgrp"])
    C.load(bexp_sb, bexp_sb.ap, IN["bexp"])
    C.load(b2_sb, b2_sb.ap, IN["b2"], q="gpsimd")
    C.load(wgrp_sb, wgrp_sb.ap.rearrange("p (c j) -> p c j", c=8),
           rearr_w(IN["wgrp"]), q="gpsimd")
    C.load(wexp_sb, wexp_sb.ap.rearrange("p (c j) -> p c j", c=8),
           rearr_w(IN["wexp"]), q="gpsimd")

    def vec_to_cols(dstb, src_row, n):
        rows = n // 128
        C.load(vec8, vec8.ap[0:rows, 0:128],
               src_row.rearrange("o (r p) -> (o r) p", p=128))
        C.transp(psC, PSC[0:128, 0:rows], vec8.ap[0:rows, 0:128],
                 idf.ap[0:rows, 0:rows], ins=(vec8, idf))
        C.CPY(dstb, dstb.ap, psC, PSC[0:128, 0:rows])

    vec_to_cols(gnw_sb, IN["gnw"], 512)
    vec_to_cols(lnw_sb, IN["ln_w"], 1024)
    vec_to_cols(lnbc_f, IN["ln_b"], 1024)
    C.CPY(lnbc, lnbc.ap, lnbc_f, lnbc_f.ap)
    vec_to_cols(bmk, IN["bi"][0:1, D:2 * D], 1024)
    for e_ in range(E):
        rows = FF // 128
        C.load(vec8, vec8.ap[0:rows, 0:128],
               IN["b1"][e_:e_ + 1, :].rearrange("o (r p) -> (o r) p", p=128))
        C.transp(psC, PSC[0:128, 0:rows], vec8.ap[0:rows, 0:128],
                 idf.ap[0:rows, 0:rows], ins=(vec8, idf))
        C.CPY(b1c, b1c.ap[:, e_ * 16:(e_ + 1) * 16], psC, PSC[0:128, 0:rows])

    def rep_row_from_dram(dstb, src_dram, n, scale=None):
        C.load(row3, row3.ap[0:1, 0:n], src_dram)
        if scale is not None:
            C.TSi(row3, row3.ap[0:1, 0:n], row3, row3.ap[0:1, 0:n], scale)
        for j in range(0, n, 512):
            w = min(512, n - j)
            C.mm_group(psC, PSC[:, 0:w],
                       [(ones_f.ap[0:1, 0:128], row3.ap[0:1, j:j + w])],
                       ins=(ones_f, row3))
            C.CPY(dstb, dstb.ap[:, j:j + w], psC, PSC[:, 0:w])

    rep_row_from_dram(bmv_rep, IN["bi"][0:1, 2 * D:3 * D], D)

    if phase_limit >= 1:
        wk = A.alloc("wk", 8 * D)
        wv = A.alloc("wv", 8 * 2 * D)
        wik = A.alloc("wik", 8 * D)
        wiv = A.alloc("wiv", 8 * D)
        r8 = lambda b_: b_.ap.rearrange("p (c j) -> p c j", c=8)
        C.load(wk, r8(wk), rearr_w(IN["Wk"]), q="gpsimd")
        C.load(wv, r8(wv), rearr_w(IN["Wv"]), q="gpsimd")
        C.load(wik, r8(wik), rearr_w(IN["wiT"][:, D:2 * D]), q="gpsimd")
        C.load(wiv, r8(wiv), rearr_w(IN["wiT"][:, 2 * D:3 * D]), q="gpsimd")

        def bias_rows(w_b, ncols, repb):
            for j in range(0, ncols, 512):
                pairs = [(lnbc.ap[:, c:c + 1],
                          w_b.ap[:, c * ncols + j:c * ncols + j + 512])
                         for c in range(8)]
                C.mm_group(psB[0], PSB[0:1, 0:512], pairs, ins=(lnbc, w_b))
                C.CPY(row3, row3.ap[0:1, 0:512], psB[0], PSB[0:1, 0:512])
                C.mm_group(psC, PSC[:, 0:512],
                           [(ones_f.ap[0:1, 0:128], row3.ap[0:1, 0:512])],
                           ins=(ones_f, row3))
                C.CPY(repb, repb.ap[:, j:j + 512], psC, PSC[:, 0:512])

        bias_rows(wk, D, brk_rep)
        bias_rows(wv, 2 * D, brv_rep)
        for c in range(8):
            C.TS(wk, wk.ap[:, c * D:(c + 1) * D], wk, wk.ap[:, c * D:(c + 1) * D],
                 lnw_sb, lnw_sb.ap[:, c:c + 1])
        for c in range(8):
            C.TS(wv, wv.ap[:, c * 2 * D:(c + 1) * 2 * D],
                 wv, wv.ap[:, c * 2 * D:(c + 1) * 2 * D],
                 lnw_sb, lnw_sb.ap[:, c:c + 1])

        xf = [A.alloc("xf0", 1024, F32), A.alloc("xf1", 1024, F32)]
        sqs = A.alloc("sqs", 1024)
        xnf = [A.alloc("xnf0", 1024, F32), A.alloc("xnf1", 1024, F32)]
        ztile = [A.alloc("ztile0", 1024), A.alloc("ztile1", 1024)]
        xnb = [A.alloc("xnb0", 1024), A.alloc("xnb1", 1024)]
        _z0 = A.alloc("zTst0", 4096)
        _x0 = A.alloc("xnTst0", 4096)
        _z1 = A.alloc("zTst1", 4096)
        _x1 = A.alloc("xnTst1", 4096)
        zTst = [_z0, _z1]
        xnTst = [_x0, _x1]
        ctb = [A.alloc("ct0", 128), A.alloc("ct1", 128)]
        stb = [A.alloc("st0", 128), A.alloc("st1", 128)]
        krst = A.alloc("krst", 1024)
        krop = A.alloc("krop", 1024)
        rs1 = A.alloc("rs1", 512)
        rs2 = A.alloc("rs2", 512)
        vrst2 = [A.alloc("vrst", 512), A.alloc("vrstb", 512)]
        mkst2 = [A.alloc("mkst0", 512), A.alloc("mkst1", 512)]
        ksq2 = [A.alloc("ksq", 512), A.alloc("ksqb", 512)]
        mvst2 = [A.alloc("mvst0", 1024), A.alloc("mvst1", 1024)]

        stc2b = Buf(stc.ap[:, 24:40], "stc2b")
        stcP = [Buf(stc.ap[:, 0:16], "stcP0"), stc2b]
        def ln_pair(xbuf, gi, store_own):
            par = gi % 2
            c = stcP[0] if par == 0 else stcP[1]
            xnfp, ztp, xbp = xnf[par], ztile[par], xnb[par]
            C.ACT(sqs, sqs.ap, xbuf, xbuf.ap, AF.Identity, accb=c,
                  acc=c.ap[:, 0:1])
            C.ACT(sqs, sqs.ap, xbuf, xbuf.ap, AF.Square, accb=c, acc=c.ap[:, 1:2])
            C.TSi(c, c.ap[:, 2:3], c, c.ap[:, 0:1], 1.0 / D)
            C.TT(c, c.ap[:, 3:4], c, c.ap[:, 2:3], c, c.ap[:, 2:3])
            C.TSi(c, c.ap[:, 4:5], c, c.ap[:, 1:2], 1.0 / D, EPS, ALU.mult, ALU.add)
            C.TT(c, c.ap[:, 5:6], c, c.ap[:, 4:5], c, c.ap[:, 3:4], ALU.subtract)
            C.ACT(c, c.ap[:, 6:7], c, c.ap[:, 5:6], AF.Sqrt)
            C.RECIP(c, c.ap[:, 7:8], c, c.ap[:, 6:7])
            C.TS(xnfp, xnfp.ap, xbuf, xbuf.ap, c, c.ap[:, 2:3], c, c.ap[:, 7:8],
                 ALU.subtract, ALU.mult)
            C.ACT(xbp, xbp.ap, xnfp, xnfp.ap, AF.Identity, accb=c,
                  acc=c.ap[:, 8:9])
            if store_own:
                C.CPY(mu1o, mu1o.ap[:, gi:gi + 1], c, c.ap[:, 2:3])
                C.CPY(rs1o, rs1o.ap[:, gi:gi + 1], c, c.ap[:, 7:8])
            C.ACT(sqs, sqs.ap, xnfp, xnfp.ap, AF.Square, accb=c, acc=c.ap[:, 9:10])
            C.TSi(c, c.ap[:, 10:11], c, c.ap[:, 8:9], 1.0 / D)
            C.TT(c, c.ap[:, 11:12], c, c.ap[:, 10:11], c, c.ap[:, 10:11])
            C.TSi(c, c.ap[:, 12:13], c, c.ap[:, 9:10], 1.0 / D, EPS, ALU.mult, ALU.add)
            C.TT(c, c.ap[:, 13:14], c, c.ap[:, 12:13], c, c.ap[:, 11:12], ALU.subtract)
            C.ACT(c, c.ap[:, 14:15], c, c.ap[:, 13:14], AF.Sqrt)
            C.RECIP(c, c.ap[:, 15:16], c, c.ap[:, 14:15])
            C.TS(ztp, ztp.ap, xnfp, xnfp.ap, c, c.ap[:, 10:11], c, c.ap[:, 15:16],
                 ALU.subtract, ALU.mult)

        tp_par = [0]

        def transpose_into(srcb, dstb, tt, colsz=512):
            for half in range(2):
                hb = psB[tp_par[0]]
                off = tp_par[0] * 512
                tp_par[0] ^= 1
                bv = PSB[:, off:off + 512].bitcast(BF16)
                for c4 in range(4):
                    c = half * 4 + c4
                    C.transp(hb, bv[:, c4 * 128:(c4 + 1) * 128],
                             srcb.ap[:, c * 128:(c + 1) * 128], idb.ap,
                             ins=(srcb, idb), first=(c4 == 0), last=(c4 == 3))
                dst = dstb.ap.rearrange("p (c t) -> p c t", t=colsz)[
                    :, half * 4:half * 4 + 4, tt * 128:(tt + 1) * 128]
                src = bv[:, 0:512].rearrange("p (c t) -> p c t", t=128)
                C.op("vector", lambda E, O=dst, I=src: E.tensor_copy(O, I),
                     ins=(hb,), outs=(dstb,), inc="ve")

        def rope6(outb, o1, o2, inb, i1, i2, cb, c_ap, sbuf_, s_ap):
            C.TT(rs1, rs1.ap, inb, i1, cb, c_ap)
            C.TT(rs2, rs2.ap, inb, i2, sbuf_, s_ap)
            C.TT(outb, o1, rs1, rs1.ap, rs2, rs2.ap, ALU.subtract)
            C.TT(rs1, rs1.ap, inb, i2, cb, c_ap)
            C.TT(rs2, rs2.ap, inb, i1, sbuf_, s_ap)
            C.TT(outb, o2, rs1, rs1.ap, rs2, rs2.ap, ALU.add)

        def proj_tok(zstageb, wbuf, ncols, tt, psget, biasrep, outb, outsl):
            for half in range(ncols // 512):
                pb, ps = psget(half)
                pairs = [(zstageb.ap.rearrange("p (c t) -> p c t", t=512)
                          [:, c, tt * 128:(tt + 1) * 128],
                          wbuf.ap[:, c * ncols + half * 512:c * ncols + half * 512 + 512])
                         for c in range(8)]
                pairs.append((ones_b.ap[0:1, 0:128],
                              biasrep.ap[0:1, half * 512:half * 512 + 512]))
                C.mm_group(pb, ps, pairs, ins=(zstageb, wbuf, biasrep))
                C.ACT(outb, outsl[:, half * 512:(half + 1) * 512], pb, ps,
                      AF.Identity)

        psrot = [psA[0], psA[1], psA[2], psA[3]]

        def emit_ln(seg, tt):
            gi = seg * 4 + tt
            xbuf = xf[gi % 2]
            C.load(xbuf, xbuf.ap, IN["xb"][gi * 128:(gi + 1) * 128, :])
            ln_pair(xbuf, gi, False)

        def emit_tr(seg, tt):
            gi = seg * 4 + tt
            transpose_into(ztile[gi % 2], zTst[seg % 2], tt)
            transpose_into(xnb[gi % 2], xnTst[seg % 2], tt)

        for tt in range(4):
            emit_ln(0, tt)
            emit_tr(0, tt)
        for seg in range(16):
            sb_z, sb_x = zTst[seg % 2], xnTst[seg % 2]
            do_ret = seg * 512 < PFX_TOK
            for tt in range(4):
                if seg + 1 < 16:
                    emit_ln(seg + 1, tt)
                gi = seg * 4 + tt
                if do_ret:
                    C.load(ctb[tt % 2], ctb[tt % 2].ap,
                           IN["cos_tok"][gi * 128:(gi + 1) * 128, :])
                    C.load(stb[tt % 2], stb[tt % 2].ap,
                           IN["sin_tok"][gi * 128:(gi + 1) * 128, :])
                    proj_tok(sb_z, wk, D, tt,
                             lambda h: (psrot[h], psrot[h].ap), brk_rep,
                             krst, krst.ap)
                    kv = krst.ap.rearrange("p (h u) -> p h u", u=256)
                    ko = krop.ap.rearrange("p (h u) -> p h u", u=256)
                    cb_ap = ctb[tt % 2].ap.unsqueeze(1).broadcast_to([128, 4, 128])
                    sb_ap = stb[tt % 2].ap.unsqueeze(1).broadcast_to([128, 4, 128])
                    rope6(krop, ko[:, :, 0:128], ko[:, :, 128:256],
                          krst, kv[:, :, 0:128], kv[:, :, 128:256],
                          ctb[tt % 2], cb_ap, stb[tt % 2], sb_ap)
                    C.stor(krtok_d.ap[gi * 128:(gi + 1) * 128, :], krop, krop.ap,
                           q="gpsimd", dstb=krtok_d)
                    for vh in range(4):
                        pb, ps = psrot[vh % 2 + 2], psrot[vh % 2 + 2].ap
                        pairs = [(sb_z.ap.rearrange("p (c t) -> p c t", t=512)
                                  [:, c, tt * 128:(tt + 1) * 128],
                                  wv.ap[:, c * 2 * D + vh * 512:
                                        c * 2 * D + vh * 512 + 512])
                                 for c in range(8)]
                        pairs.append((ones_b.ap[0:1, 0:128],
                                      brv_rep.ap[0:1, vh * 512:vh * 512 + 512]))
                        C.mm_group(pb, ps, pairs, ins=(sb_z, wv, brv_rep))
                        vrp = vrst2[vh % 2]
                        C.ACT(vrp, vrp.ap, pb, ps, AF.Identity)
                        C.stor(vrtok_d.ap[gi * 128:(gi + 1) * 128,
                                          vh * 512:(vh + 1) * 512],
                               vrp, vrp.ap, q="gpsimd", dstb=vrtok_d)
                mvp = mvst2[tt % 2]
                for half in range(2):
                    pb, ps = psrot[half], psrot[half].ap
                    pairs = [(sb_x.ap.rearrange("p (c t) -> p c t", t=512)
                              [:, c, tt * 128:(tt + 1) * 128],
                              wiv.ap[:, c * D + half * 512:c * D + half * 512 + 512])
                             for c in range(8)]
                    pairs.append((ones_b.ap[0:1, 0:128],
                                  bmv_rep.ap[0:1, half * 512:half * 512 + 512]))
                    C.mm_group(pb, ps, pairs, ins=(sb_x, wiv, bmv_rep))
                    C.ACT(mvp, mvp.ap[:, half * 512:(half + 1) * 512], pb, ps,
                          AF.Identity)
                C.stor(mv_d.ap[gi * 128:(gi + 1) * 128, :], mvp, mvp.ap,
                       q="gpsimd", dstb=mv_d)
                if seg + 1 < 16:
                    emit_tr(seg + 1, tt)
            if dbg and seg == 0:
                C.stor(dbgd["dbg_a"].ap, sb_x, sb_x.ap, q="gpsimd",
                       dstb=dbgd["dbg_a"])
                C.stor(dbgd["dbg_b"].ap, sb_z, sb_z.ap, q="gpsimd",
                       dstb=dbgd["dbg_b"])
            def norm_step(kc):
                kq = ksq2[kc % 2]
                if kc % 2 == 0:
                    pc, pcs, tp = psC, PSC[0:2, 0:512], (0, 0)
                else:
                    pc, pcs, tp = psC2, PSC[64:66, 0:512], (0, 64)
                C.op("tensor", lambda E, P=pcs, L=psel.ap, R=kq.ap, t=tp:
                     E.matmul(P, L, R, start=True, stop=True, tile_position=t),
                     ins=(psel, kq), outs=(pc,), inc="pe")
                C.RED(maxk_sb, maxk_sb.ap[:, kc * 16 + seg:kc * 16 + seg + 1],
                      pc, pcs)
            for kc in range(8):
                pb, ps = psrot[kc % 4], psrot[kc % 4].ap
                pairs = [(wik.ap[:, c * D + kc * 128:c * D + kc * 128 + 128],
                          sb_x.ap[:, c * 512:(c + 1) * 512]) for c in range(8)]
                C.mm_group(pb, ps, pairs, ins=(wik, sb_x))
                mkp = mkst2[kc % 2]
                C.ACT(mkp, mkp.ap, pb, ps, AF.Identity,
                      biasb=bmk, bias=bmk.ap[:, kc:kc + 1])
                C.stor(mkT_d.ap[kc * 128:(kc + 1) * 128, seg * 512:(seg + 1) * 512],
                       mkp, mkp.ap, q="gpsimd", dstb=mkT_d)
                C.ACT(ksq2[kc % 2], ksq2[kc % 2].ap, mkp, mkp.ap, AF.Square)
                if kc > 0:
                    norm_step(kc - 1)
            norm_step(7)

        A.freebuf("krst", "krop", "vrst", "vrstb", "mkst0", "mkst1",
                  "ksq", "ksqb",
                  "mvst0", "mvst1",
                  "wk", "wv", "wik", "wiv", "zTst0", "xnTst0", "zTst1", "xnTst1")

        zT_own = A.alloc("zT_own", 8 * TL)
        xnT_own = A.alloc("xnT_own", 8 * TL)
        def emit_ln15(gi):
            xbuf = xf[gi % 2]
            C.load(xbuf, xbuf.ap, IN["x_own"][gi * 128:(gi + 1) * 128, :])
            ln_pair(xbuf, gi, True)
        emit_ln15(0)
        for gi in range(16):
            if gi + 1 < 16:
                emit_ln15(gi + 1)
            transpose_into(ztile[gi % 2], zT_own, gi, colsz=TL)
            transpose_into(xnb[gi % 2], xnT_own, gi, colsz=TL)
        A.freebuf("xf0", "xf1", "sqs", "xnf0", "xnf1", "ztile0", "ztile1",
                  "xnb0", "xnb1", "ct0", "ct1", "st0", "st1",
                  "rs1", "rs2")


    if phase_limit >= 2:
        r8 = lambda b_, n_: b_.ap.rearrange("p (c j) -> p c j", c=8)

        def bias_cols(w_b, ncols, dstb, njc):
            for jc in range(njc):
                pairs = [(w_b.ap[:, c * ncols + jc * 128:c * ncols + jc * 128 + 128],
                          lnbc.ap[:, c:c + 1]) for c in range(8)]
                C.mm_group(psC, PSC[:, 0:1], pairs, ins=(w_b, lnbc))
                C.CPY(dstb, dstb.ap[:, jc:jc + 1], psC, PSC[:, 0:1])

        wg = A.alloc("wg", 8 * 2 * D)
        C.load(wg, r8(wg, 2 * D), rearr_w(IN["Wg"]), q="gpsimd")
        bias_cols(wg, 2 * D, brg, 16)
        for c in range(8):
            C.TS(wg, wg.ap[:, c * 2 * D:(c + 1) * 2 * D],
                 wg, wg.ap[:, c * 2 * D:(c + 1) * 2 * D],
                 lnw_sb, lnw_sb.ap[:, c:c + 1])
        gst = A.alloc("gst", TL)
        for gc in range(16):
            for ts4 in range(4):
                pb = psA[ts4]
                pairs = [(wg.ap[:, c * 2 * D + gc * 128:c * 2 * D + gc * 128 + 128],
                          zT_own.ap[:, c * TL + ts4 * 512:c * TL + ts4 * 512 + 512])
                         for c in range(8)]
                C.mm_group(pb, pb.ap, pairs, ins=(wg, zT_own))
                C.ACT(gst, gst.ap[:, ts4 * 512:(ts4 + 1) * 512],
                      pb, pb.ap, AF.Identity, biasb=brg, bias=brg.ap[:, gc:gc + 1])
            C.stor(gT_d.ap[gc * 128:(gc + 1) * 128, :], gst, gst.ap, q="gpsimd",
                   dstb=gT_d)
        A.freebuf("wg", "gst")

        wq = A.alloc("wq", 8 * D)
        C.load(wq, r8(wq, D), rearr_w(IN["Wq"]), q="gpsimd")
        bias_cols(wq, D, brq, 8)
        C.TSi(brq, brq.ap, brq, brq.ap, 1.0 / 16.0)
        for c in range(8):
            C.TS(wq, wq.ap[:, c * D:(c + 1) * D], wq, wq.ap[:, c * D:(c + 1) * D],
                 lnw_sb, lnw_sb.ap[:, c:c + 1], None, 1.0 / 16.0, ALU.mult, ALU.mult)
        qrT = A.alloc("qrT", 8 * TL)
        rf = [A.alloc(f"rf{i}", TL) for i in range(4)]
        cosf = A.alloc("cosf", TL)
        sinf = A.alloc("sinf", TL)
        C.load(cosf, cosf.ap, IN["cos_f"])
        C.load(sinf, sinf.ap, IN["sin_f"])
        for qc in range(8):
            for ts4 in range(4):
                pb = psA[ts4]
                pairs = [(wq.ap[:, c * D + qc * 128:c * D + qc * 128 + 128],
                          zT_own.ap[:, c * TL + ts4 * 512:c * TL + ts4 * 512 + 512])
                         for c in range(8)]
                C.mm_group(pb, pb.ap, pairs, ins=(wq, zT_own))
                C.ACT(qrT, qrT.ap[:, qc * TL + ts4 * 512:qc * TL + ts4 * 512 + 512],
                      pb, pb.ap, AF.Identity, biasb=brq, bias=brq.ap[:, qc:qc + 1])
        for h in range(H):
            q1 = qrT.ap[:, (h * 2) * TL:(h * 2) * TL + TL]
            q2 = qrT.ap[:, (h * 2 + 1) * TL:(h * 2 + 1) * TL + TL]
            C.TT(rf[0], rf[0].ap, qrT, q1, cosf, cosf.ap)
            C.TT(rf[1], rf[1].ap, qrT, q2, sinf, sinf.ap)
            C.TT(rf[2], rf[2].ap, qrT, q2, cosf, cosf.ap)
            C.TT(rf[3], rf[3].ap, qrT, q1, sinf, sinf.ap)
            C.TT(qrT, q1, rf[0], rf[0].ap, rf[1], rf[1].ap, ALU.subtract)
            C.TT(qrT, q2, rf[2], rf[2].ap, rf[3], rf[3].ap, ALU.add)
        A.freebuf("wq", "rf0", "rf1", "rf2", "rf3", "cosf", "sinf")

        wk2 = A.alloc("wk2", 8 * D)
        wv2 = A.alloc("wv2", 8 * 2 * D)
        C.load(wk2, r8(wk2, D), rearr_w(IN["Wk"]), q="gpsimd")
        C.load(wv2, r8(wv2, 2 * D), rearr_w(IN["Wv"]), q="gpsimd")
        for c in range(8):
            C.TS(wk2, wk2.ap[:, c * D:(c + 1) * D], wk2, wk2.ap[:, c * D:(c + 1) * D],
                 lnw_sb, lnw_sb.ap[:, c:c + 1])
        for c in range(8):
            C.TS(wv2, wv2.ap[:, c * 2 * D:(c + 1) * 2 * D],
                 wv2, wv2.ap[:, c * 2 * D:(c + 1) * 2 * D],
                 lnw_sb, lnw_sb.ap[:, c:c + 1])
        if dbg:
            C.stor(dbgd["dbg_a"].ap, xnT_own, xnT_own.ap[:, 0:4096], q="gpsimd",
                   dstb=dbgd["dbg_a"])
        vec_to_cols(bmq, IN["bi"][0:1, 0:D], 1024)
        C.TSi(bmq, bmq.ap, bmq, bmq.ap, 0.125)
        if dbg:
            C.stor(dbgd["dbg_b"].ap, xnT_own, xnT_own.ap[:, 0:4096], q="gpsimd",
                   dstb=dbgd["dbg_b"])

        krst2 = A.alloc("krst2", 1024)
        krop2 = A.alloc("krop2", 1024)
        vrst2 = A.alloc("vrst2", 2048)
        rr1 = A.alloc("rr1", 512)
        rr2 = A.alloc("rr2", 512)
        ct2 = [A.alloc("ct2a", 128), A.alloc("ct2b", 128)]
        st2 = [A.alloc("st2a", 128), A.alloc("st2b", 128)]
        for gi in range(16):
            C.load(ct2[gi % 2], ct2[gi % 2].ap, IN["cos_own"][gi * 128:(gi + 1) * 128, :])
            C.load(st2[gi % 2], st2[gi % 2].ap, IN["sin_own"][gi * 128:(gi + 1) * 128, :])
            for half in range(2):
                pb = psA[half]
                pairs = [(zT_own.ap[:, c * TL + gi * 128:c * TL + gi * 128 + 128],
                          wk2.ap[:, c * D + half * 512:c * D + half * 512 + 512])
                         for c in range(8)]
                C.mm_group(pb, pb.ap, pairs, ins=(zT_own, wk2))
                C.TT(krst2, krst2.ap[:, half * 512:(half + 1) * 512], pb, pb.ap,
                     brk_rep, brk_rep.ap[:, half * 512:(half + 1) * 512], ALU.add)
            kv = krst2.ap.rearrange("p (h u) -> p h u", u=256)
            ko = krop2.ap.rearrange("p (h u) -> p h u", u=256)
            cb_ap = ct2[gi % 2].ap.unsqueeze(1).broadcast_to([128, 4, 128])
            sb_ap = st2[gi % 2].ap.unsqueeze(1).broadcast_to([128, 4, 128])
            C.TT(rr1, rr1.ap.rearrange("p (h u) -> p h u", u=128), krst2,
                 kv[:, :, 0:128], ct2[gi % 2], cb_ap)
            C.TT(rr2, rr2.ap.rearrange("p (h u) -> p h u", u=128), krst2,
                 kv[:, :, 128:256], st2[gi % 2], sb_ap)
            C.TT(krop2, ko[:, :, 0:128], rr1,
                 rr1.ap.rearrange("p (h u) -> p h u", u=128), rr2,
                 rr2.ap.rearrange("p (h u) -> p h u", u=128), ALU.subtract)
            C.TT(rr1, rr1.ap.rearrange("p (h u) -> p h u", u=128), krst2,
                 kv[:, :, 128:256], ct2[gi % 2], cb_ap)
            C.TT(rr2, rr2.ap.rearrange("p (h u) -> p h u", u=128), krst2,
                 kv[:, :, 0:128], st2[gi % 2], sb_ap)
            C.TT(krop2, ko[:, :, 128:256], rr1,
                 rr1.ap.rearrange("p (h u) -> p h u", u=128), rr2,
                 rr2.ap.rearrange("p (h u) -> p h u", u=128), ALU.add)
            C.stor(krown_d.ap[gi * 128:(gi + 1) * 128, :], krop2, krop2.ap,
                   q="gpsimd", dstb=krown_d)
            for half in range(4):
                pb = psA[half]
                pairs = [(zT_own.ap[:, c * TL + gi * 128:c * TL + gi * 128 + 128],
                          wv2.ap[:, c * 2 * D + half * 512:c * 2 * D + half * 512 + 512])
                         for c in range(8)]
                C.mm_group(pb, pb.ap, pairs, ins=(zT_own, wv2))
                C.TT(vrst2, vrst2.ap[:, half * 512:(half + 1) * 512], pb, pb.ap,
                     brv_rep, brv_rep.ap[:, half * 512:(half + 1) * 512], ALU.add)
            C.stor(vrown_d.ap[gi * 128:(gi + 1) * 128, :], vrst2, vrst2.ap,
                   q="gpsimd", dstb=vrown_d)
        if dbg:
            C.stor(dbgd["dbg_c"].ap, xnT_own, xnT_own.ap[:, 0:4096], q="gpsimd",
                   dstb=dbgd["dbg_c"])
        A.freebuf("wk2", "wv2", "krst2", "krop2", "vrst2", "rr1", "rr2",
                  "ct2a", "ct2b", "st2a", "st2b", "zT_own")
        wiq = A.alloc("wiq", 8 * D)
        C.load(wiq, r8(wiq, D), rearr_w(IN["wiT"][:, 0:D]), q="gpsimd")
        C.TSi(wiq, wiq.ap, wiq, wiq.ap, 0.125)
        qmT = A.alloc("qmT", 8 * TL)
        qmsq = A.alloc("qmsq", 512)
        for qc in range(8):
            for ts4 in range(4):
                pb = psA[ts4]
                pairs = [(wiq.ap[:, c * D + qc * 128:c * D + qc * 128 + 128],
                          xnT_own.ap[:, c * TL + ts4 * 512:c * TL + ts4 * 512 + 512])
                         for c in range(8)]
                C.mm_group(pb, pb.ap, pairs, ins=(wiq, xnT_own))
                C.ACT(qmT, qmT.ap[:, qc * TL + ts4 * 512:qc * TL + ts4 * 512 + 512],
                      pb, pb.ap, AF.Identity, biasb=bmq, bias=bmq.ap[:, qc:qc + 1])
                C.ACT(qmsq, qmsq.ap, qmT,
                      qmT.ap[:, qc * TL + ts4 * 512:qc * TL + ts4 * 512 + 512],
                      AF.Square)
                C.mm_group(psC, PSC[0:2, 0:512], [(psel.ap, qmsq.ap)],
                           ins=(psel, qmsq))
                C.RED(maxq_sb, maxq_sb.ap[:, qc * 4 + ts4:qc * 4 + ts4 + 1],
                      psC, PSC[0:2, 0:512])
        for qc in range(8):
            C.stor(qmT_d.ap[qc * 128:(qc + 1) * 128, :], qmT,
                   qmT.ap[:, qc * TL:(qc + 1) * TL], q="gpsimd", dstb=qmT_d)
        A.freebuf("wiq", "qmsq", "xnT_own", "qmT")

        S_sb = A.alloc("S_sb", H * 2 * 512, F32)
        S_bf = A.alloc("S_bf", H * 2 * 512)
        krp = [A.alloc("krp0", 2048), A.alloc("krp1", 2048)]
        vrp = [A.alloc("vrp0", 4096), A.alloc("vrp1", 4096)]
        ksdp = A.alloc("ksdp", 512)
        for hp in range(2):
            started = [[False] * 2 for _ in range(2)]
            for m in range(NPFX):
                kb, vb = krp[m % 2], vrp[m % 2]
                C.load(kb, kb.ap.rearrange("p (a j) -> p a j", a=2),
                       IN_krp := krtok_d.ap[m * 256:(m + 1) * 256, :].rearrange(
                           "(a p) j -> p a j", p=128), srcb=krtok_d)
                C.load(vb, vb.ap.rearrange("p (a j) -> p a j", a=2),
                       vrtok_d.ap[m * 256:(m + 1) * 256, :].rearrange(
                           "(a p) j -> p a j", p=128), srcb=vrtok_d)
                for hh in range(2):
                    h = hp * 2 + hh
                    for kt in range(2):
                        C.TS(ksdp, ksdp.ap[:, hh * 256:(hh + 1) * 256], kb,
                             kb.ap[:, kt * 1024 + h * 256:kt * 1024 + h * 256 + 256],
                             pfx_sb, pfx_sb.ap[:, (m * H + h) * 2 + kt:
                                               (m * H + h) * 2 + kt + 1])
                        for c in range(2):
                            pslot = psA[hh * 2 + c]
                            st_ = not started[hh][c]
                            started[hh][c] = True
                            last = (m == NPFX - 1 and kt == 1)
                            C.op("tensor",
                                 lambda E, P=pslot.ap, L=ksdp.ap[
                                     :, hh * 256 + c * 128:hh * 256 + c * 128 + 128],
                                 R=vb.ap[:, kt * 2048 + h * 512:
                                         kt * 2048 + h * 512 + 512],
                                 a=st_, b=last: E.matmul(P, L, R, start=a, stop=b),
                                 ins=(ksdp, vb), outs=(pslot,) if last else (),
                                 inc="pe")
            for hh in range(2):
                h = hp * 2 + hh
                for c in range(2):
                    blkslice = slice((h * 2 + c) * 512, (h * 2 + c) * 512 + 512)
                    C.CPY(S_sb, S_sb.ap[:, blkslice], psA[hh * 2 + c],
                          psA[hh * 2 + c].ap)
        for h in range(H):
            C.CPY(S_bf, S_bf.ap[:, h * 1024:(h + 1) * 1024], S_sb,
                  S_sb.ap[:, h * 1024:(h + 1) * 1024])

        A.freebuf("krp0", "krp1", "vrp0", "vrp1", "ksdp")
        dmask_sb = A.alloc("dmask_sb", H * 2 * CP, F32)
        for r in range(8):
            C.load(dmask_sb, dmask_sb.ap[:, r * 256:(r + 1) * 256],
                   IN["dmaskT"][r * 128:(r + 1) * 128, :])
        crep = A.alloc("crep", H * CP)
        C.load(crep, crep.ap, IN["crossrep"])
        krT_sc = A.alloc("krT_sc", 2048)
        qs_sc = A.alloc("qs_sc", 2048)
        ksd_sc = A.alloc("ksd_sc", 512)
        pt_sc = A.alloc("pt_sc", 512)
        of32 = A.alloc("of32", 1024, F32)
        osq = A.alloc("osq", 1024)
        rrep_sb = A.alloc("rrep_sb", 256, F32)
        ont = A.alloc("ont", 1024)
        sil = A.alloc("sil", 1024)
        ow_sc = A.alloc("ow_sc", 4096)
        gsb = A.alloc("gsb", 4096)
        kro = [A.alloc("kro0", 2048), A.alloc("kro1", 2048)]
        vro = [A.alloc("vro0", 4096), A.alloc("vro1", 4096)]
        for sc in range(NSC):
            kb, vb = kro[sc % 2], vro[sc % 2]
            C.load(kb, kb.ap.rearrange("p (a j) -> p a j", a=2),
                   krown_d.ap[sc * 256:(sc + 1) * 256, :].rearrange(
                       "(a p) j -> p a j", p=128), srcb=krown_d)
            C.load(vb, vb.ap.rearrange("p (a j) -> p a j", a=2),
                   vrown_d.ap[sc * 256:(sc + 1) * 256, :].rearrange(
                       "(a p) j -> p a j", p=128), srcb=vrown_d)
            for gc in range(16):
                C.load(gsb, gsb.ap[:, gc * 256:(gc + 1) * 256],
                       gT_d.ap[gc * 128:(gc + 1) * 128, sc * 256:(sc + 1) * 256],
                       srcb=gT_d)
            for rnd in range(2):
                for i4 in range(4):
                    idx = rnd * 4 + i4
                    h, c = idx // 2, idx % 2
                    for kt in range(2):
                        C.transp(psT, PST[:, (i4 * 2 + kt) * 128:
                                          (i4 * 2 + kt) * 128 + 128],
                                 kb.ap[:, kt * 1024 + h * 256 + c * 128:
                                       kt * 1024 + h * 256 + c * 128 + 128],
                                 idb.ap, ins=(kb, idb),
                                 first=(i4 == 0 and kt == 0),
                                 last=(i4 == 3 and kt == 1))
                dst = krT_sc.ap[:, rnd * 1024:(rnd + 1) * 1024]
                C.op("vector", lambda E, O=dst, I=PST[:, 0:1024]:
                     E.tensor_copy(O, I), ins=(psT,), outs=(krT_sc,), inc="ve")
            for idx in range(8):
                h, c = idx // 2, idx % 2
                C.TT(qs_sc, qs_sc.ap[:, idx * 256:(idx + 1) * 256], qrT,
                     qrT.ap[:, idx * TL + sc * 256:idx * TL + sc * 256 + 256],
                     crep, crep.ap[:, h * CP:(h + 1) * CP])
            for h in range(H):
                for kt in range(2):
                    pairs = [(krT_sc.ap[:, (h * 2 + c) * 256 + kt * 128:
                                        (h * 2 + c) * 256 + kt * 128 + 128],
                              qrT.ap[:, (h * 2 + c) * TL + sc * 256:
                                     (h * 2 + c) * TL + sc * 256 + 256])
                             for c in range(2)]
                    C.mm_group(psB[kt], PSB[:, kt * 512:kt * 512 + 256], pairs,
                               ins=(krT_sc, qrT))
                    C.TT(pt_sc, pt_sc.ap[:, kt * 256:(kt + 1) * 256], psB[kt],
                         PSB[:, kt * 512:kt * 512 + 256], dmask_sb,
                         dmask_sb.ap[:, (h * 2 + kt) * 256:(h * 2 + kt) * 256 + 256])
                for dvc in range(4):
                    after = []
                    pslot_ap = PSA[:, dvc * 256:(dvc + 1) * 256]
                    psbuf = psA[dvc // 2]
                    seqs = []
                    for kt in range(2):
                        seqs.append((vb.ap[:, kt * 2048 + h * 512 + dvc * 128:
                                           kt * 2048 + h * 512 + dvc * 128 + 128],
                                     pt_sc.ap[:, kt * 256:(kt + 1) * 256]))
                    for c in range(2):
                        seqs.append((S_bf.ap[:, (h * 2 + c) * 512 + dvc * 128:
                                             (h * 2 + c) * 512 + dvc * 128 + 128],
                                     qs_sc.ap[:, (h * 2 + c) * 256:
                                              (h * 2 + c) * 256 + 256]))
                    n_ = len(seqs)
                    afterb = [vb.wr, pt_sc.wr, S_bf.wr, qs_sc.wr, psbuf.wr]
                    afterb.extend(psbuf.rd)
                    tok = None
                    for i_, (L, R) in enumerate(seqs):
                        tok = S.emit("tensor",
                                     lambda E, P=pslot_ap, LL=L, RR=R,
                                     a=(i_ == 0), b=(i_ == n_ - 1):
                                     E.matmul(P, LL, RR, start=a, stop=b),
                                     inc="pe" if i_ == n_ - 1 else None,
                                     after=afterb if i_ == 0 else ())
                    for b_ in (vb, pt_sc, S_bf, qs_sc):
                        b_.rd.append(tok)
                    psbuf.wr = tok
                    psbuf.rd = []
                for kt in range(2):
                    C.TS(ksd_sc, ksd_sc.ap[:, kt * 256:(kt + 1) * 256], kb,
                         kb.ap[:, kt * 1024 + h * 256:kt * 1024 + h * 256 + 256],
                         kdec_sb, kdec_sb.ap[:, h * 2 + kt:h * 2 + kt + 1])
                for c in range(2):
                    pairs = [(ksd_sc.ap[:, kt * 256 + c * 128:kt * 256 + c * 128 + 128],
                              vb.ap[:, kt * 2048 + h * 512:kt * 2048 + h * 512 + 512])
                             for kt in range(2)]
                    C.mm_group(psA[2 + c], PSA[:, (2 + c) * 512:(3 + c) * 512],
                               pairs, ins=(ksd_sc, vb))
                C.op("vector", lambda E, O=of32.ap, I=PSA[:, 0:1024]:
                     E.tensor_copy(O, I), ins=(psA[0], psA[1]), outs=(of32,),
                     inc="ve")
                for c in range(2):
                    C.STT(S_sb, S_sb.ap[:, (h * 2 + c) * 512:(h * 2 + c) * 512 + 512],
                          S_sb, S_sb.ap[:, (h * 2 + c) * 512:(h * 2 + c) * 512 + 512],
                          GC[h], psA[2 + c], PSA[:, (2 + c) * 512:(3 + c) * 512],
                          ALU.mult, ALU.add)
                C.CPY(S_bf, S_bf.ap[:, h * 1024:(h + 1) * 1024], S_sb,
                      S_sb.ap[:, h * 1024:(h + 1) * 1024])
                C.ACT(osq, osq.ap, of32, of32.ap, AF.Square)
                pairs = [(onescol.ap, osq.ap[:, dvc * 256:(dvc + 1) * 256])
                         for dvc in range(4)]
                C.mm_group(psC, PSC[0:1, 0:256], pairs, ins=(onescol, osq))
                C.TSi(drow, drow.ap[0:1, 0:256], psC, PSC[0:1, 0:256],
                      1.0 / DV, EPS, ALU.mult, ALU.add)
                C.ACT(drow, drow.ap[0:1, 256:512], drow, drow.ap[0:1, 0:256],
                      AF.Sqrt)
                C.RECIP(drow, drow.ap[0:1, 512:768], drow, drow.ap[0:1, 256:512])
                C.mm_group(psC, PSC[:, 256:512],
                           [(ones_f.ap[0:1, 0:128], drow.ap[0:1, 512:768])],
                           ins=(ones_f, drow))
                C.CPY(rrep_sb, rrep_sb.ap, psC, PSC[:, 256:512])
                for dvc in range(4):
                    C.TT(ont, ont.ap[:, dvc * 256:(dvc + 1) * 256], of32,
                         of32.ap[:, dvc * 256:(dvc + 1) * 256], rrep_sb, rrep_sb.ap)
                    C.TS(ont, ont.ap[:, dvc * 256:(dvc + 1) * 256], ont,
                         ont.ap[:, dvc * 256:(dvc + 1) * 256], gnw_sb,
                         gnw_sb.ap[:, dvc:dvc + 1])
                C.ACT(sil, sil.ap, gsb, gsb.ap[:, h * 1024:(h + 1) * 1024], AF.Silu)
                C.TT(ow_sc, ow_sc.ap[:, h * 1024:(h + 1) * 1024], ont, ont.ap,
                     sil, sil.ap)
            for rc in range(16):
                C.stor(ow_d.ap[rc * 128:(rc + 1) * 128, sc * 256:(sc + 1) * 256],
                       ow_sc, ow_sc.ap[:, rc * 256:(rc + 1) * 256], q="gpsimd",
                       dstb=ow_d)
        A.freebuf("S_sb", "S_bf",
                  "dmask_sb", "crep", "krT_sc", "qs_sc", "ksd_sc", "pt_sc",
                  "of32", "osq", "rrep_sb", "ont", "sil", "ow_sc", "gsb",
                  "kro0", "kro1", "vro0", "vro1", "qrT")

        wo_sb = A.alloc("wo_sb", 16 * D)
        C.load(wo_sb, wo_sb.ap.rearrange("p (c j) -> p c j", c=16),
               IN["Wo"].rearrange("(c p) j -> p c j", p=128), q="gpsimd")
        owt = A.alloc("owt", 2048)
        rout = A.alloc("rout", 1024, F32)
        for tt in range(16):
            C.load(owt, owt.ap.rearrange("p (rc t) -> p rc t", rc=16),
                   ow_d.ap[:, tt * 128:(tt + 1) * 128].rearrange(
                       "(rc p) t -> p rc t", p=128), srcb=ow_d)
            for jt in range(2):
                pairs = [(owt.ap[:, rc * 128:(rc + 1) * 128],
                          wo_sb.ap[:, rc * D + jt * 512:rc * D + jt * 512 + 512])
                         for rc in range(16)]
                C.mm_group(psA[jt], psA[jt].ap, pairs, ins=(owt, wo_sb))
                C.CPY(rout, rout.ap[:, jt * 512:(jt + 1) * 512], psA[jt],
                      psA[jt].ap)
            C.stor(retout_d.ap[tt * 128:(tt + 1) * 128, :], rout, rout.ap,
                   q="gpsimd", dstb=retout_d)
        A.freebuf("wo_sb", "owt", "rout")


    if phase_limit >= 3:
        bo_rep = A.alloc("bo_rep", D, F32)
        l1g_rep = A.alloc("l1g_rep", D, F32)
        l1b_rep = A.alloc("l1b_rep", D, F32)
        rep_row_from_dram(bo_rep, IN["bo_v"], D)
        rep_row_from_dram(l1g_rep, IN["l1g"], D)
        rep_row_from_dram(l1b_rep, IN["l1b"], D)

        qmT = A.alloc("qmT3", 8 * TL)
        for qc in range(8):
            C.load(qmT, qmT.ap[:, qc * TL:(qc + 1) * TL],
                   qmT_d.ap[qc * 128:(qc + 1) * 128, :], srcb=qmT_d)
        c = stc
        C.RED(c, c.ap[0:2, 8:16], maxq_sb,
              maxq_sb.ap.rearrange("r (c4 s) -> r c4 s", s=4))
        C.RED(c, c.ap[0:2, 16:24], maxk_sb,
              maxk_sb.ap.rearrange("r (c8 s) -> r c8 s", s=16))
        C.ACT(c, c.ap[0:2, 8:16], c, c.ap[0:2, 8:16], AF.Sqrt)
        C.ACT(c, c.ap[0:2, 16:24], c, c.ap[0:2, 16:24], AF.Sqrt)
        C.TT(c, c.ap[0:2, 0:8], c, c.ap[0:2, 8:16], c, c.ap[0:2, 16:24])
        C.TSi(c, c.ap[0:2, 0:8], c, c.ap[0:2, 0:8], -1.0)
        C.transp(psC, PSC[0:8, 16:18], c.ap[0:2, 0:8], idf.ap[0:2, 0:2],
                 ins=(stc, idf))
        C.CPY(c, c.ap[0:8, 4:6], psC, PSC[0:8, 16:18])
        C.stor(row_d.ap[0:1, 0:16].rearrange("o (q r) -> (o q) r", r=2),
               c, c.ap[0:8, 4:6], dstb=row_d)
        C.load(c, c.ap[0:1, 0:16], row_d.ap[0:1, 0:16], srcb=row_d)
        C.mm_group(psC, PSC[:, 0:16],
                   [(ones_f.ap[0:1, 0:128], c.ap[0:1, 0:16])], ins=(ones_f, c))
        C.CPY(mh_bias, mh_bias.ap[:, 0:NH], psC, PSC[:, 0:16])
        mv3 = mh_bias.ap[:, 0:NH].rearrange("p (hp two) -> p hp two", two=2)
        C.op("vector", lambda E, O=mh_bias.ap[:, NH:NH + 8],
             A_=mv3[:, :, 0:1].squeeze(2), B_=mv3[:, :, 1:2].squeeze(2):
             E.tensor_tensor(O, A_, B_, ALU.min),
             ins=(mh_bias,), outs=(mh_bias,), inc="ve")

        dacc_a = A.alloc("dacc_a", TL, F32)
        C.memset(dacc_a, dacc_a.ap[0:16, :], 0.0, eng="vector", inc="ve")
        o_acc = A.alloc("o_acc", 8 * TL, F32)
        kTb = [A.alloc("kT0", 4096), A.alloc("kT1", 4096)]
        vtb = [A.alloc("vt0", 4096), A.alloc("vt1", 4096)]
        PtP = [A.alloc("PtP0", 1024), A.alloc("PtP1", 1024)]
        psST = [Buf(PSA[:, i * 512:(i + 1) * 512], f"psST{i}") for i in range(4)]
        psPV = Buf(PSB[:, 0:512], "psPV")
        psDA = Buf(PSC[0:1, 0:512], "psDA")
        psDB = Buf(PSC[64:65, 0:512], "psDB")

        def load_seg(kdst, vdst, seg):
            for kc in range(8):
                C.load(kdst, kdst.ap[:, kc * 512:(kc + 1) * 512],
                       mkT_d.ap[kc * 128:(kc + 1) * 128, seg * 512:(seg + 1) * 512],
                       srcb=mkT_d)
            for ti in range(4):
                C.load(vdst, vdst.ap[:, ti * 1024:(ti + 1) * 1024],
                       mv_d.ap[seg * 512 + ti * 128:seg * 512 + (ti + 1) * 128, :],
                       srcb=mv_d)

        for sp in range(8):
            a_b = [(kTb[0], vtb[0]), (kTb[1], vtb[1])]
            load_seg(a_b[0][0], a_b[0][1], sp * 2)
            load_seg(a_b[1][0], a_b[1][1], sp * 2 + 1)
            for hp in range(NH // 2):
                hA, hB = 2 * hp, 2 * hp + 1
                for qt in range(4):
                    q0 = qt * 512

                    def emit_st(gk):
                        s2, kt = gk // 4, gk % 4
                        kT_, _ = a_b[s2]
                        sA, sB = psST[(gk % 2) * 2], psST[(gk % 2) * 2 + 1]
                        ptp = PtP[gk % 2]
                        for (base, ps_) in ((0, sA), (64, sB)):
                            C.mm_group(ps_, ps_.ap,
                                       [(kT_.ap[base:base + 64,
                                                hp * 512 + kt * 128:
                                                hp * 512 + kt * 128 + 128],
                                         qmT.ap[base:base + 64,
                                                hp * TL + q0:hp * TL + q0 + 512])],
                                       ins=(kT_, qmT))
                        C.op("scalar", lambda E, O=ptp.ap,
                             I=PSA[:, (gk % 2) * 1024:(gk % 2 + 1) * 1024],
                             b=mh_bias.ap[:, NH + hp:NH + hp + 1]:
                             E.activation(O, I, AF.Exp, bias=b),
                             ins=(sA, sB, mh_bias), outs=(ptp,), inc="ac")

                    def emit_pv(gk):
                        s2, kt = gk // 4, gk % 4
                        _, vt_ = a_b[s2]
                        ptp = PtP[gk % 2]
                        pta = Buf(ptp.ap[:, 0:512], "pta"); pta.wr = ptp.wr
                        ptb = Buf(ptp.ap[:, 512:1024], "ptb"); ptb.wr = ptp.wr
                        pta.rd = ptp.rd; ptb.rd = ptp.rd
                        st_, sp_ = (gk == 0), (gk == 7)
                        exPV = ([psPV.wr] + psPV.rd) if st_ else []
                        exD = ([psDA.wr, psDB.wr] + psDA.rd + psDB.rd) \
                            if st_ else []
                        C.op("tensor", lambda E, P=PSB[0:64, 0:512],
                             L=vt_.ap[:, kt * 1024 + hp * 128:
                                      kt * 1024 + hp * 128 + 64],
                             R=pta.ap, a=st_, b=sp_:
                             E.matmul(P, L, R, start=a, stop=b,
                                      tile_position=(0, 0)),
                             ins=(vt_, pta), outs=(), inc="pe", extra=exPV)
                        C.op("tensor", lambda E, P=PSB[64:128, 0:512],
                             L=vt_.ap[:, kt * 1024 + hp * 128 + 64:
                                      kt * 1024 + hp * 128 + 128],
                             R=ptb.ap, a=st_, b=sp_:
                             E.matmul(P, L, R, start=a, stop=b,
                                      tile_position=(0, 64)),
                             ins=(vt_, ptb), outs=(psPV,) if sp_ else (),
                             inc="pe")
                        C.op("tensor", lambda E, P=psDA.ap, L=onescol.ap,
                             R=pta.ap, a=st_, b=sp_:
                             E.matmul(P, L, R, start=a, stop=b),
                             ins=(onescol, pta), outs=(), inc="pe", extra=exD)
                        C.op("tensor", lambda E, P=psDB.ap, L=onescol.ap,
                             R=ptb.ap, a=st_, b=sp_:
                             E.matmul(P, L, R, start=a, stop=b,
                                      tile_position=(0, 64)),
                             ins=(onescol, ptb), outs=(psDB,) if sp_ else (),
                             inc="pe")

                    emit_st(0)
                    for gk in range(1, 8):
                        emit_st(gk)
                        emit_pv(gk - 1)
                    emit_pv(7)
                    dst = o_acc.ap[:, hp * 2048 + q0:hp * 2048 + q0 + 512]
                    if sp == 0:
                        C.op("vector", lambda E, O=dst, I=PSB[:, 0:512]:
                             E.tensor_copy(O, I), ins=(psPV,), outs=(o_acc,),
                             inc="ve")
                    else:
                        C.op("vector", lambda E, O=dst, I=PSB[:, 0:512], X=dst:
                             E.tensor_tensor(O, I, X, ALU.add),
                             ins=(psPV,), outs=(o_acc,), inc="ve")
                    C.op("scalar", lambda E, O=dstage.ap[0:1, q0:q0 + 512],
                         I=psDA.ap: E.activation(O, I, AF.Identity),
                         ins=(psDA, psDB), outs=(dstage,), inc="ac")
                    C.op("scalar", lambda E, O=dstage2.ap[0:1, q0:q0 + 512],
                         I=psDB.ap: E.activation(O, I, AF.Identity),
                         ins=(psDA, psDB), outs=(dstage2,), inc="ac")
                C.stor(dacc_a.ap[hA:hA + 1, :], dstage, dstage.ap[0:1, :],
                       q="gpsimd", dstb=dacc_a, accum=True)
                C.stor(dacc_a.ap[hB:hB + 1, :], dstage2, dstage2.ap[0:1, :],
                       q="gpsimd", dstb=dacc_a, accum=True)
        A.freebuf("kT0", "kT1", "vt0", "vt1", "PtP0", "PtP1")
        C.RECIP(dacc_a, dacc_a.ap[0:16, :], dacc_a, dacc_a.ap[0:16, :])
        on_bf = A.alloc("on_bf", 8 * TL)
        rrep3 = A.alloc("rrep3", TL, F32)
        for h in range(NH):
            hc, hr = h // 2, h % 2
            C.load(drow, drow.ap[0:1, :], dacc_a.ap[h:h + 1, :], srcb=dacc_a)
            for j4 in range(4):
                C.mm_group(psC, PSC[:, 0:512],
                           [(ones_f.ap[0:1, 0:128],
                             drow.ap[0:1, j4 * 512:(j4 + 1) * 512])],
                           ins=(ones_f, drow))
                C.CPY(rrep3, rrep3.ap[:, j4 * 512:(j4 + 1) * 512], psC, PSC[:, 0:512])
            C.op("vector", lambda E,
                 O=on_bf.ap[hr * 64:hr * 64 + 64, hc * 2048:(hc + 1) * 2048],
                 I=o_acc.ap[hr * 64:hr * 64 + 64, hc * 2048:(hc + 1) * 2048],
                 X=rrep3.ap[hr * 64:hr * 64 + 64, :]:
                 E.tensor_tensor(O, I, X, ALU.mult),
                 ins=(o_acc, rrep3), outs=(on_bf,), inc="ve")
        A.freebuf("o_acc", "dacc_a", "rrep3", "qmT3")

        woT_sb = A.alloc("woT_sb", 8 * D)
        C.load(woT_sb, woT_sb.ap.rearrange("p (c j) -> p c j", c=8),
               rearr_w(IN["woT"]), q="gpsimd")
        h1T = A.alloc("h1T", 8 * TL)
        xot2 = [A.alloc("xot", 1024, F32), A.alloc("xotb", 1024, F32)]
        uf2 = [A.alloc("uf", 1024, F32), A.alloc("ufb", 1024, F32)]
        xnt2 = [A.alloc("xnt", 1024, F32), A.alloc("xntb", 1024, F32)]
        tf2 = [A.alloc("tf", 1024, F32), A.alloc("tfb", 1024, F32)]
        sq32 = [A.alloc("sq3", 1024), A.alloc("sq3b", 1024)]
        h1n2 = [A.alloc("h1n", 1024, F32), A.alloc("h1nb", 1024, F32)]
        h1b32 = [A.alloc("h1b3", 1024), A.alloc("h1b3b", 1024)]
        cL = [Buf(stc.ap[:, 16:24], "cL0"), Buf(stc.ap[:, 24:32], "cL1")]
        for tt in range(16):
            p = tt % 2
            xot, uf, xnt, tf = xot2[p], uf2[p], xnt2[p], tf2[p]
            sq3, h1n, h1b3, cp = sq32[p], h1n2[p], h1b32[p], cL[p]
            pj = (psA[0], psA[1]) if p == 0 else (psA[2], psA[3])
            pslc = PSA[:, 0:1024] if p == 0 else PSA[:, 1024:2048]
            C.load(xot, xot.ap, IN["x_own"][tt * 128:(tt + 1) * 128, :])
            for j in range(2):
                pairs = [(on_bf.ap[:, rc * 2048 + tt * 128:rc * 2048 + tt * 128 + 128],
                          woT_sb.ap[:, rc * D + j * 512:rc * D + j * 512 + 512])
                         for rc in range(8)]
                C.mm_group(pj[j], pj[j].ap, pairs, ins=(on_bf, woT_sb))
            C.op("vector", lambda E, O=uf.ap, I=pslc, X=bo_rep.ap:
                 E.tensor_tensor(O, I, X, ALU.add),
                 ins=(pj[0], pj[1], bo_rep), outs=(uf,), inc="ve")
            C.TS(xnt, xnt.ap, xot, xot.ap, mu1o, mu1o.ap[:, tt:tt + 1],
                 rs1o, rs1o.ap[:, tt:tt + 1], ALU.subtract, ALU.mult)
            C.STT(tf, tf.ap, uf, uf.ap, 1.0, xnt, xnt.ap, ALU.mult, ALU.add,
                  accb=cp, acc=cp.ap[:, 0:1])
            C.ACT(sq3, sq3.ap, tf, tf.ap, AF.Square, accb=cp, acc=cp.ap[:, 1:2])
            C.TSi(cp, cp.ap[:, 2:3], cp, cp.ap[:, 0:1], 1.0 / D)
            C.TT(cp, cp.ap[:, 3:4], cp, cp.ap[:, 2:3], cp, cp.ap[:, 2:3])
            C.TSi(cp, cp.ap[:, 4:5], cp, cp.ap[:, 1:2], 1.0 / D, EPS,
                  ALU.mult, ALU.add)
            C.TT(cp, cp.ap[:, 5:6], cp, cp.ap[:, 4:5], cp, cp.ap[:, 3:4],
                 ALU.subtract)
            C.ACT(cp, cp.ap[:, 6:7], cp, cp.ap[:, 5:6], AF.Sqrt)
            C.RECIP(cp, cp.ap[:, 7:8], cp, cp.ap[:, 6:7])
            C.TS(h1n, h1n.ap, tf, tf.ap, cp, cp.ap[:, 2:3], cp, cp.ap[:, 7:8],
                 ALU.subtract, ALU.mult)
            C.TT(h1n, h1n.ap, h1n, h1n.ap, l1g_rep, l1g_rep.ap)
            C.TT(h1n, h1n.ap, h1n, h1n.ap, l1b_rep, l1b_rep.ap, ALU.add)
            C.stor(h1_d.ap[tt * 128:(tt + 1) * 128, :], h1n, h1n.ap, dstb=h1_d)
            C.CPY(h1b3, h1b3.ap, h1n, h1n.ap)
            transpose_into(h1b3, h1T, tt, colsz=TL)
        A.freebuf("on_bf", "woT_sb", "xot", "uf", "xnt", "tf", "sq3", "h1n",
                  "h1b3", "xotb", "ufb", "xntb", "tfb", "sq3b", "h1nb",
                  "h1b3b", "bo_rep", "l1g_rep", "l1b_rep")


    if phase_limit >= 4:
        l2g_rep = A.alloc("l2g_rep", D, F32)
        l2b_rep = A.alloc("l2b_rep", D, F32)
        rep_row_from_dram(l2g_rep, IN["l2g"], D)
        rep_row_from_dram(l2b_rep, IN["l2b"], D)
        c = stc
        gl4 = A.alloc("gl4", TL, F32)
        el8 = A.alloc("el8", TL, F32)
        for ts4 in range(4):
            pairs = [(wgrp_sb.ap[:, cc * 4:(cc + 1) * 4],
                      h1T.ap[:, cc * TL + ts4 * 512:cc * TL + ts4 * 512 + 512])
                     for cc in range(8)]
            C.mm_group(psC, PSC[0:4, 0:512], pairs, ins=(wgrp_sb, h1T))
            C.op("scalar", lambda E, O=gl4.ap[0:4, ts4 * 512:(ts4 + 1) * 512],
                 I=PSC[0:4, 0:512], b=bgrp_sb.ap:
                 E.activation(O, I, AF.Identity, bias=b),
                 ins=(psC, bgrp_sb), outs=(gl4,), inc="ac")
            pairs = [(wexp_sb.ap[:, cc * 8:(cc + 1) * 8],
                      h1T.ap[:, cc * TL + ts4 * 512:cc * TL + ts4 * 512 + 512])
                     for cc in range(8)]
            C.mm_group(psC, PSC[0:8, 0:512], pairs, ins=(wexp_sb, h1T))
            C.op("scalar", lambda E, O=el8.ap[0:8, ts4 * 512:(ts4 + 1) * 512],
                 I=PSC[0:8, 0:512], b=bexp_sb.ap:
                 E.activation(O, I, AF.Identity, bias=b),
                 ins=(psC, bexp_sb), outs=(el8,), inc="ac")
        gtok = A.alloc("gtok", 64, F32)
        etok = A.alloc("etok", 128, F32)
        for tt in range(16):
            C.transp(psC, PSC[0:128, 0:4], gl4.ap[0:4, tt * 128:(tt + 1) * 128],
                     idf.ap[0:4, 0:4], ins=(gl4, idf))
            C.CPY(gtok, gtok.ap[:, tt * 4:(tt + 1) * 4], psC, PSC[0:128, 0:4])
            C.transp(psC, PSC[0:128, 4:12], el8.ap[0:8, tt * 128:(tt + 1) * 128],
                     idf.ap[0:8, 0:8], ins=(el8, idf))
            C.CPY(etok, etok.ap[:, tt * 8:(tt + 1) * 8], psC, PSC[0:128, 4:12])
        A.freebuf("gl4", "el8")
        comb8 = A.alloc("comb8", 128, F32)
        r4 = A.alloc("r4", 8, F32)
        for tt in range(16):
            g4 = gtok.ap[:, tt * 4:(tt + 1) * 4]
            C.RED(c, c.ap[:, 0:1], gtok, g4, alu=ALU.max, negate=True)
            C.op("scalar", lambda E, O=r4.ap[:, 0:4], I=g4,
                 b=c.ap[:, 0:1]: E.activation(O, I, AF.Exp, bias=b),
                 ins=(gtok, stc), outs=(r4,), inc="ac")
            C.RED(c, c.ap[:, 1:2], r4, r4.ap[:, 0:4], alu=ALU.add)
            C.RED(c, c.ap[:, 2:3], r4, r4.ap[:, 0:4], alu=ALU.max)
            C.RECIP(c, c.ap[:, 3:4], c, c.ap[:, 1:2])
            C.TT(c, c.ap[:, 4:5], c, c.ap[:, 2:3], c, c.ap[:, 3:4])
            C.TS(r4, r4.ap[:, 4:8], r4, r4.ap[:, 0:4],
                 c, c.ap[:, 2:3], None, None, ALU.is_equal)
            ev = etok.ap.rearrange("p (t g) -> p t g", g=8)[:, tt, :].rearrange(
                "p (g2 two) -> p g2 two", two=2)
            C.op("vector", lambda E, O=c.ap[:, 8:12], A_=ev[:, :, 0].squeeze() if
                 False else ev[:, :, 0:1].squeeze(2),
                 B_=ev[:, :, 1:2].squeeze(2):
                 E.tensor_tensor(O, A_, B_, ALU.subtract),
                 ins=(etok,), outs=(stc,), inc="ve")
            C.TSi(c, c.ap[:, 12:16], c, c.ap[:, 8:12], 0.0, None, ALU.is_lt)
            C.op("scalar", lambda E, O=c.ap[:, 16:20], I=c.ap[:, 8:12]:
                 E.activation(O, I, AF.Abs), ins=(stc,), outs=(stc,), inc="ac")
            C.op("scalar", lambda E, O=c.ap[:, 16:20], I=c.ap[:, 16:20]:
                 E.activation(O, I, AF.Sigmoid), ins=(stc,), outs=(stc,), inc="ac")
            C.TT(c, c.ap[:, 20:24], stc, c.ap[:, 16:20], r4, r4.ap[:, 4:8])
            C.RED(c, c.ap[:, 5:6], stc, c.ap[:, 20:24], alu=ALU.add)
            C.TT(c, c.ap[:, 6:7], c, c.ap[:, 4:5], c, c.ap[:, 5:6])
            cm8 = comb8.ap.rearrange("p (t g) -> p t g", g=8)[:, tt, :]
            cm2 = cm8.rearrange("p (g2 two) -> p g2 two", two=2)
            C.op("vector", lambda E, O=cm2[:, :, 1:2].squeeze(2), A_=r4.ap[:, 4:8],
                 B_=c.ap[:, 12:16]: E.tensor_tensor(O, A_, B_, ALU.mult),
                 ins=(r4, stc), outs=(comb8,), inc="ve")
            C.op("vector", lambda E, O=cm2[:, :, 0:1].squeeze(2), A_=r4.ap[:, 4:8],
                 B_=cm2[:, :, 1:2].squeeze(2):
                 E.tensor_tensor(O, A_, B_, ALU.subtract),
                 ins=(r4, comb8), outs=(comb8,), inc="ve")
            C.TS(comb8, cm8, comb8, cm8, c, c.ap[:, 6:7], None, None, ALU.mult)
        combT = A.alloc("combT", TL)
        for tt in range(16):
            C.transp(psC, PSC[0:8, 0:128],
                     comb8.ap.rearrange("p (t g) -> p t g", g=8)[:, tt, :],
                     idf.ap, ins=(comb8, idf))
            C.CPY(combT, combT.ap[0:8, tt * 128:(tt + 1) * 128], psC, PSC[0:8, 0:128])
        A.freebuf("gtok", "etok", "r4", "comb8")
        crepb = A.alloc("crepb", TL)

        w1q = [A.alloc("w1q0", 8192), A.alloc("w1q1", 8192)]
        w2q = [A.alloc("w2q0", 8192), A.alloc("w2q1", 8192)]
        hbt = A.alloc("hbt", 8 * TL)
        yst = [A.alloc("yst0", 1024, F32), A.alloc("yst1", 1024, F32)]
        y_tiles = [Buf(y_d.ap[tt * 128:(tt + 1) * 128, :], f"y{tt}")
                   for tt in range(16)]
        for e_ in range(E):
            for j4 in range(4):
                C.mm_group(psC, PSC[:, 0:512],
                           [(sel8_sb.ap[:, e_ * 128:(e_ + 1) * 128],
                             combT.ap[0:8, j4 * 512:(j4 + 1) * 512])],
                           ins=(sel8_sb, combT))
                C.CPY(crepb, crepb.ap[:, j4 * 512:(j4 + 1) * 512], psC, PSC[:, 0:512])
            for hf in range(2):
                idx = e_ * 2 + hf
                wb = w1q[idx % 2]
                if e_ == 0 and hf == 0:
                    C.load(wb, wb.ap.rearrange("p (cc j) -> p cc j", cc=8),
                           IN["w1"][e_].rearrange("(cc p) j -> p cc j", p=128)
                           [:, :, hf * 1024:(hf + 1) * 1024], q="gpsimd")
                wb2 = w2q[idx % 2]
                C.load(wb2, wb2.ap.rearrange("p (cc j) -> p cc j", cc=8),
                       IN["w2"][e_][hf * 1024:(hf + 1) * 1024, :].rearrange(
                           "(cc p) j -> p cc j", p=128), q="gpsimd")
                for tg in range(4):
                    tok0 = tg * 512
                    for fc in range(8):
                        pairs = [(wb.ap[:, cc * 1024 + fc * 128:
                                        cc * 1024 + fc * 128 + 128],
                                  h1T.ap[:, cc * TL + tok0:cc * TL + tok0 + 512])
                                 for cc in range(8)]
                        C.mm_group(psB[fc % 2],
                                   PSB[:, (fc % 2) * 512:(fc % 2 + 1) * 512],
                                   pairs, ins=(wb, h1T))
                        fci = hf * 8 + fc
                        C.op("scalar", lambda E,
                             O=hbt.ap[:, fc * TL + tok0:fc * TL + tok0 + 512],
                             I=PSB[:, (fc % 2) * 512:(fc % 2 + 1) * 512],
                             b=b1c.ap[:, e_ * 16 + fci:e_ * 16 + fci + 1]:
                             E.activation(O, I, AF.Relu, bias=b),
                             ins=(psB[fc % 2], b1c), outs=(hbt,), inc="ac")
                        C.op("vector", lambda E,
                             O=hbt.ap[:, fc * TL + tok0:fc * TL + tok0 + 512],
                             A_=hbt.ap[:, fc * TL + tok0:fc * TL + tok0 + 512],
                             B_=crepb.ap[:, tok0:tok0 + 512]:
                             E.tensor_tensor(O, A_, B_, ALU.mult),
                             ins=(hbt, crepb), outs=(hbt,), inc="ve")
                nidx = idx + 1
                if nidx < 16:
                    ne, nhf = nidx // 2, nidx % 2
                    nwb = w1q[nidx % 2]
                    C.load(nwb, nwb.ap.rearrange("p (cc j) -> p cc j", cc=8),
                           IN["w1"][ne].rearrange("(cc p) j -> p cc j", p=128)
                           [:, :, nhf * 1024:(nhf + 1) * 1024], q="gpsimd")
                for t8 in range(8):
                    tok0 = t8 * 256
                    for t1 in range(2):
                        tt = t8 * 2 + t1
                        for dj in range(2):
                            slot = psA[t1 * 2 + dj]
                            after = [wb2.wr, hbt.wr, combT.wr, b2_sb.wr,
                                     slot.wr] + slot.rd
                            tok = None
                            if e_ == 0 and hf == 0:
                                tok = S.emit(
                                    "tensor",
                                    lambda E, P=slot.ap,
                                    L=combT.ap[0:8, tt * 128:tt * 128 + 128],
                                    R=b2_sb.ap[0:8, dj * 512:(dj + 1) * 512]:
                                    E.matmul(P, L, R, start=True, stop=False),
                                    inc="pe", after=after)
                            for fc in range(8):
                                tok = S.emit(
                                    "tensor",
                                    lambda E, P=slot.ap,
                                    L=hbt.ap[:, fc * TL + tok0 + t1 * 128:
                                             fc * TL + tok0 + t1 * 128 + 128],
                                    R=wb2.ap[:, fc * 1024 + dj * 512:
                                             fc * 1024 + dj * 512 + 512],
                                    a=(fc == 0 and not (e_ == 0 and hf == 0)),
                                    b=(fc == 7):
                                    E.matmul(P, L, R, start=a, stop=b),
                                    inc="pe",
                                    after=after if fc == 0 else ())
                            for b_ in (wb2, hbt, combT, b2_sb):
                                b_.rd.append(tok)
                            slot.wr = tok
                            slot.rd = []
                        ys = yst[tt % 2]
                        C.op("vector", lambda E, O=ys.ap,
                             I=PSA[:, (t1 * 2) * 512:(t1 * 2 + 2) * 512]:
                             E.tensor_copy(O, I),
                             ins=(psA[t1 * 2], psA[t1 * 2 + 1]), outs=(ys,),
                             inc="ve")
                        C.stor(y_tiles[tt].ap, ys, ys.ap,
                               q="gpsimd", dstb=y_tiles[tt],
                               accum=not (e_ == 0 and hf == 0))
        A.freebuf("w1q0", "w1q1", "w2q0", "w2q1", "hbt", "crepb", "combT",
                  "h1T", "yst0", "yst1")

        yac = A.alloc("yac", 1024, F32)
        ret4 = A.alloc("ret4", 1024, F32)
        xo4 = A.alloc("xo4", 1024, F32)
        h14 = A.alloc("h14", 1024, F32)
        sq4 = A.alloc("sq4", 1024)
        ffn4 = A.alloc("ffn4", 1024, F32)
        ot4 = A.alloc("ot4", 1024, F32)
        for tt in range(16):
            C.load(ret4, ret4.ap, retout_d.ap[tt * 128:(tt + 1) * 128, :],
                   srcb=retout_d)
            C.load(xo4, xo4.ap, IN["x_own"][tt * 128:(tt + 1) * 128, :])
            C.load(h14, h14.ap, h1_d.ap[tt * 128:(tt + 1) * 128, :], srcb=h1_d)
            C.load(yac, yac.ap, y_tiles[tt].ap, srcb=y_tiles[tt])
            C.TT(yac, yac.ap, yac, yac.ap, h14, h14.ap, ALU.add)
            C.RED(c, c.ap[:, 16:17], yac, yac.ap, alu=ALU.add)
            C.ACT(sq4, sq4.ap, yac, yac.ap, AF.Square, accb=c, acc=c.ap[:, 17:18])
            C.TSi(c, c.ap[:, 18:19], c, c.ap[:, 16:17], 1.0 / D)
            C.TT(c, c.ap[:, 19:20], c, c.ap[:, 18:19], c, c.ap[:, 18:19])
            C.TSi(c, c.ap[:, 20:21], c, c.ap[:, 17:18], 1.0 / D, EPS,
                  ALU.mult, ALU.add)
            C.TT(c, c.ap[:, 21:22], c, c.ap[:, 20:21], c, c.ap[:, 19:20],
                 ALU.subtract)
            C.ACT(c, c.ap[:, 22:23], c, c.ap[:, 21:22], AF.Sqrt)
            C.RECIP(c, c.ap[:, 23:24], c, c.ap[:, 22:23])
            C.TS(ffn4, ffn4.ap, yac, yac.ap, c, c.ap[:, 18:19],
                 c, c.ap[:, 23:24], ALU.subtract, ALU.mult)
            C.TT(ffn4, ffn4.ap, ffn4, ffn4.ap, l2g_rep, l2g_rep.ap)
            C.TT(ffn4, ffn4.ap, ffn4, ffn4.ap, l2b_rep, l2b_rep.ap, ALU.add)
            C.TS(xo4, xo4.ap, xo4, xo4.ap, mu1o, mu1o.ap[:, tt:tt + 1],
                 rs1o, rs1o.ap[:, tt:tt + 1], ALU.subtract, ALU.mult)
            C.load(ot4, ot4.ap, IN["x_own"][tt * 128:(tt + 1) * 128, :])
            C.TT(ot4, ot4.ap, ot4, ot4.ap, xo4, xo4.ap, ALU.add)
            C.TT(ot4, ot4.ap, ot4, ot4.ap, ret4, ret4.ap, ALU.add)
            C.TT(ot4, ot4.ap, ot4, ot4.ap, ffn4, ffn4.ap, ALU.add)
            C.stor(out_d[tt * 128:(tt + 1) * 128, :], ot4, ot4.ap, dstb=out_b)
        A.freebuf("yac", "ret4", "xo4", "h14", "sq4", "ffn4", "ot4")



    if phase_limit < 4:
        fin = A.alloc("fin", 1024, F32)
        for i in range(16):
            C.load(fin, fin.ap, IN["x_own"][i * 128:(i + 1) * 128, :])
            C.stor(out_d[i * 128:(i + 1) * 128, :], fin, fin.ap, dstb=out_b)
    S.barrier()

    with nc.Block() as blk:
        S.replay(blk, sems)
    es.close()
    return nc


def host_tables(core):
    s = core % 4
    i = np.arange(0, DK, 2, dtype=np.float32) / DK
    inv_freq = (10000.0 ** -i).astype(np.float32)
    t_idx = np.arange(T, dtype=np.float32)
    fr = t_idx[:, None] * inv_freq[None, :]
    cos = np.cos(fr).astype(NPBF)
    sin = np.sin(fr).astype(NPBF)
    own = slice(s * TL, (s + 1) * TL)
    g = np.float32(GAMMA)
    ii = np.arange(CP, dtype=np.float32)
    dm = np.where(ii[:, None] >= ii[None, :],
                  g[:, None, None] ** (ii[:, None] - ii[None, :]), 0.0
                  ).astype(np.float32)
    dmT = dm.transpose(0, 2, 1)
    dmaskT = np.ascontiguousarray(
        dmT.reshape(H, 2, 128, CP)).reshape(H * 2 * 128, CP)
    crossrep = np.broadcast_to((g[:, None] ** (ii + 1.0)).astype(NPBF)
                               .reshape(1, H * CP), (128, H * CP))
    kdec = np.zeros((128, H * 2), np.float32)
    for h in range(H):
        for half in range(2):
            p = np.arange(128, dtype=np.float32) + half * 128
            kdec[:, h * 2 + half] = g[h] ** (CP - 1.0 - p)
    pfx = np.zeros((128, NPFX * H * 2), np.float32)
    for m in range(NPFX):
        for h in range(H):
            for half in range(2):
                p = np.arange(128, dtype=np.float32) + half * 128
                if m < 8 * s:
                    w = g[h] ** ((CP - 1.0 - p) + CP * (8 * s - 1 - m))
                else:
                    w = np.zeros(128, np.float32)
                pfx[:, (m * H + h) * 2 + half] = w
    sel8 = np.zeros((8, 8 * 128), NPBF)
    for e in range(8):
        sel8[e, e * 128:(e + 1) * 128] = 1.0
    pairsel_np = np.zeros((128, 2), NPBF)
    pairsel_np[0:64, 0] = 1.0
    pairsel_np[64:128, 1] = 1.0
    return {
        "cos_tok": np.ascontiguousarray(cos),
        "sin_tok": np.ascontiguousarray(sin),
        "cos_own": np.ascontiguousarray(cos[own]),
        "sin_own": np.ascontiguousarray(sin[own]),
        "cos_f": np.ascontiguousarray(cos[own].T),
        "sin_f": np.ascontiguousarray(sin[own].T),
        "dmaskT": dmaskT,
        "crossrep": np.ascontiguousarray(crossrep),
        "kdec_c": kdec,
        "pfx_c": pfx,
        "ident_b": np.eye(128, dtype=NPBF),
        "ident_f": np.eye(128, dtype=np.float32),
        "ones_row_b": np.ones((1, 512), NPBF),
        "ones_row_f": np.ones((1, 128), np.float32),
        "ones_col_b": np.ones((128, 1), NPBF),
        "pairsel": pairsel_np,
        "selrow8": sel8,
        "zeros_col": np.zeros((128, 1), np.float32),
    }


def make_inmaps(inputs):
    f32 = lambda k: np.ascontiguousarray(np.asarray(inputs[k], np.float32))
    x = f32("L0_attn_x")
    base = {
        "Wq": f32("L0_attn_Wq"), "Wk": f32("L0_attn_Wk"),
        "Wv": f32("L0_attn_Wv"), "Wg": f32("L0_attn_Wg"),
        "Wo": f32("L0_attn_Wo"),
        "wiT": np.ascontiguousarray(f32("L0_ffn_in_proj_w").T),
        "woT": np.ascontiguousarray(f32("L0_ffn_out_proj_w").T),
        "w1": f32("L0_ffn_w1"), "w2": f32("L0_ffn_w2"),
        "wgrp": f32("L0_ffn_w_group"),
        "wexp": np.ascontiguousarray(
            f32("L0_ffn_w_expert").transpose(1, 0, 2).reshape(D, E)),
        "ln_w": f32("L0_attn_ln_w").reshape(1, D),
        "ln_b": f32("L0_attn_ln_b").reshape(1, D),
        "gnw": f32("L0_attn_g_norm_w").reshape(1, DV),
        "bi": f32("L0_ffn_in_proj_b").reshape(1, 3 * D),
        "bo_v": f32("L0_ffn_out_proj_b").reshape(1, D),
        "l1g": f32("L0_ffn_ln1_g").reshape(1, D),
        "l1b": f32("L0_ffn_ln1_b").reshape(1, D),
        "l2g": f32("L0_ffn_ln2_g").reshape(1, D),
        "l2b": f32("L0_ffn_ln2_b").reshape(1, D),
        "bgrp": f32("L0_ffn_b_group").reshape(4, 1),
        "bexp": f32("L0_ffn_b_expert").reshape(E, 1),
        "b1": f32("L0_ffn_b1"), "b2": f32("L0_ffn_b2"),
    }
    maps = []
    for core in range(8):
        b, s = core // 4, core % 4
        m = dict(base)
        m["xb"] = np.ascontiguousarray(x[b])
        m["x_own"] = np.ascontiguousarray(x[b, s * TL:(s + 1) * TL])
        m.update(host_tables(core))
        maps.append(m)
    return maps


def kernel(**inputs):
    maps = make_inmaps(inputs)
    nc = build_nc()
    res = run_bass_kernel_spmd(nc, maps, core_ids=list(range(8)))
    parts = [res.results[c]["out"] for c in range(8)]
    outp = np.empty((B, T, D), np.float32)
    for core in range(8):
        b, s = core // 4, core % 4
        outp[b, s * TL:(s + 1) * TL] = parts[core]
    return outp


# version 13 (final)
# speedup vs baseline: 1.1661x; 1.1661x over previous
import math
from contextlib import ExitStack

import numpy as np
import ml_dtypes

import concourse.bass as bass
import concourse.mybir as mybir
from concourse.bass_utils import run_bass_kernel_spmd

F32 = mybir.dt.float32
BF16 = mybir.dt.bfloat16
AF = mybir.ActivationFunctionType
ALU = mybir.AluOpType
AX = mybir.AxisListType
NPBF = ml_dtypes.bfloat16

B, T, D = 2, 8192, 1024
TL = 2048
H, DK, DV = 4, 256, 512
CP = 256
NSC = TL // CP
NPFX = (T - TL) // CP
PFX_TOK = T - TL
NH, HD = 16, 64
E, FF = 8, 2048
EPS = 1e-5
GAMMA = [1.0 - 2.0 ** (-5.0 - h) for h in range(H)]
GC = [float(np.float32(g) ** CP) for g in GAMMA]

PHASE_LIMIT = 4
DBG = False


class Sched:
    ENGS = ("tensor", "vector", "scalar", "gpsimd", "sync")

    def __init__(self):
        self.q = {e: [] for e in self.ENGS}
        self.cnt = {}
        self.last_wait = {}

    def _reg(self, sem):
        self.cnt.setdefault(sem, 0)

    def emit(self, eng, fn, inc=None, amt=1, after=()):
        seen = {}
        for tok in after:
            if tok is None:
                continue
            sem, val = tok
            if val > seen.get(sem, 0):
                seen[sem] = val
        for sem, val in seen.items():
            self._reg(sem)
            if self.last_wait.get((eng, sem), 0) >= val:
                continue
            self.last_wait[(eng, sem)] = val
            self.q[eng].append(("w", sem, val))
        tok = None
        if inc is not None:
            self._reg(inc)
            self.cnt[inc] += amt
            tok = (inc, self.cnt[inc])
        self.q[eng].append(("o", fn, [] if inc is None else [(inc, amt)]))
        return tok

    def tok(self, sem):
        self._reg(sem)
        return (sem, self.cnt[sem])

    def barrier(self):
        toks = [self.tok(s) for s in list(self.cnt)]
        for e in self.ENGS:
            for t_ in toks:
                self.emit(e, lambda E: E.nop(), after=[t_])

    def replay(self, block, sems):
        def mk(name):
            def run(Eng):
                for item in self.q[name]:
                    if item[0] == "w":
                        Eng.wait_ge(sems[item[1]], item[2])
                    else:
                        ins = item[1](Eng)
                        for sem, amt in item[2]:
                            ins.then_inc(sems[sem], amt)
            return run
        block.tensor(mk("tensor"))
        block.vector(mk("vector"))
        block.scalar(mk("scalar"))
        block.gpsimd(mk("gpsimd"))
        block.sync(mk("sync"))


class Buf:
    __slots__ = ("ap", "wr", "rd", "name")

    def __init__(self, ap, name=""):
        self.ap = ap
        self.wr = None
        self.rd = []
        self.name = name

    def __getitem__(self, key):
        return self.ap[key]


class Ctx:

    def __init__(self, S):
        self.S = S

    def op(self, eng, fn, ins=(), outs=(), inc=None, amt=1, extra=()):
        after = list(extra)
        for b in ins:
            after.append(b.wr)
        for b in outs:
            after.append(b.wr)
            after.extend(b.rd)
        tok = self.S.emit(eng, fn, inc=inc, amt=amt, after=after)
        for b in ins:
            if tok is not None:
                b.rd.append(tok)
        for b in outs:
            b.wr = tok
            b.rd = []
        return tok

    def mm(self, ps, lhsT, rhs, start, stop, psbuf, ins=(), inc=None):
        return self.op("tensor",
                       lambda E, p=ps, l=lhsT, r=rhs, a=start, b=stop:
                       E.matmul(p, l, r, start=a, stop=b),
                       ins=ins, outs=(psbuf,) if stop else (),
                       inc=inc if stop else None)

    def mm_group(self, psbuf, ps_ap, pairs, ins=(), inc="pe"):
        n = len(pairs)
        after = []
        for b in ins:
            after.append(b.wr)
        after.append(psbuf.wr)
        after.extend(psbuf.rd)
        tok = None
        for i, (l, r) in enumerate(pairs):
            tok = self.S.emit("tensor",
                              lambda E, p=ps_ap, L=l, R=r, a=(i == 0), b=(i == n - 1):
                              E.matmul(p, L, R, start=a, stop=b),
                              inc=inc if i == n - 1 else None,
                              after=after if i == 0 else ())
        for b in ins:
            b.rd.append(tok)
        psbuf.wr = tok
        psbuf.rd = []
        return tok

    def transp(self, psbuf, ps_ap, in_ap, id_ap, ins=(), inc="pe", last=True,
               first=True):
        after = []
        for b in ins:
            after.append(b.wr)
        if first:
            after.append(psbuf.wr)
            after.extend(psbuf.rd)
        tok = self.S.emit("tensor",
                          lambda E, p=ps_ap, i=in_ap, d=id_ap: E.transpose(p, i, d),
                          inc=inc if last else None, after=after if first else ())
        if last:
            for b in ins:
                b.rd.append(tok)
            psbuf.wr = tok
            psbuf.rd = []
        return tok

    def TT(self, outb, o, a_b, a, b_b, b_ap, alu=ALU.mult, inc="ve", eng="vector"):
        return self.op(eng, lambda E, O=o, X=a, Y=b_ap, P=alu:
                       E.tensor_tensor(O, X, Y, P),
                       ins=(a_b, b_b), outs=(outb,), inc=inc)

    def TS(self, outb, o, inb, i, s1b, s1, s2b=None, s2=None, op0=ALU.mult,
           op1=ALU.mult, accb=None, acc=None, inc="ve", eng="vector"):
        assert not (acc is not None and s2 is not None), \
            "tensor_scalar: 2-op + accum_out drops op1 on HW"
        if s2 is None:
            op1 = ALU.bypass
        ins = [inb, s1b] + ([s2b] if s2b is not None else [])
        outs = [outb] + ([accb] if accb is not None else [])
        return self.op(eng, lambda E, O=o, I=i, A=s1, Bs=s2, p=op0, q=op1, ac=acc:
                       E.tensor_scalar(O, I, A, Bs, p, q, accum_out=ac),
                       ins=ins, outs=outs, inc=inc)

    def TSi(self, outb, o, inb, i, s1, s2=None, op0=ALU.mult, op1=ALU.mult,
            accb=None, acc=None, inc="ve", eng="vector"):
        if s2 is None:
            op1 = ALU.bypass
        outs = [outb] + ([accb] if accb is not None else [])
        return self.op(eng, lambda E, O=o, I=i, A=s1, Bs=s2, p=op0, q=op1, ac=acc:
                       E.tensor_scalar(O, I, A, Bs, p, q, accum_out=ac),
                       ins=(inb,), outs=outs, inc=inc)

    def STT(self, outb, o, i0b, i0, s, i1b, i1, op0, op1, accb=None, acc=None,
            sb_=None, inc="ve", eng="vector"):
        ins = [i0b, i1b] + ([sb_] if sb_ is not None else [])
        outs = [outb] + ([accb] if accb is not None else [])
        return self.op(eng, lambda E, O=o, A=i0, c=s, Bt=i1, p=op0, q=op1, ac=acc:
                       E.scalar_tensor_tensor(O, A, c, Bt, p, q, accum_out=ac),
                       ins=ins, outs=outs, inc=inc)

    def ACT(self, outb, o, inb, i, f, biasb=None, bias=0.0, scale=1.0,
            accb=None, acc=None, inc="ac"):
        ins = [inb] + ([biasb] if biasb is not None else [])
        outs = [outb] + ([accb] if accb is not None else [])
        return self.op("scalar", lambda E, O=o, I=i, F_=f, b=bias, s=scale, ac=acc:
                       E.activation(O, I, F_, bias=b, scale=s, accum_out=ac),
                       ins=ins, outs=outs, inc=inc)

    def CPY(self, outb, o, inb, i, inc="ve", eng="vector"):
        return self.op(eng, lambda E, O=o, I=i: E.tensor_copy(O, I),
                       ins=(inb,), outs=(outb,), inc=inc)

    def RED(self, outb, o, inb, i, alu=ALU.max, negate=False, inc="ve"):
        return self.op("vector", lambda E, O=o, I=i, P=alu, n=negate:
                       E.tensor_reduce(O, I, AX.X, P, negate=n),
                       ins=(inb,), outs=(outb,), inc=inc)

    def RECIP(self, outb, o, inb, i, inc="ve"):
        return self.op("vector", lambda E, O=o, I=i: E.reciprocal(O, I),
                       ins=(inb,), outs=(outb,), inc=inc)

    def load(self, outb, o, in_ap, q="sync", srcb=None):
        ins = (srcb,) if srcb is not None else ()
        return self.op(q, lambda E, O=o, I=in_ap: E.dma_start(out=O, in_=I),
                       ins=ins, outs=(outb,), inc="d0" if q == "sync" else "d1",
                       amt=16)

    def stor(self, out_ap, inb, i, q="sync", dstb=None, accum=False):
        outs = (dstb,) if dstb is not None else ()
        if accum:
            fn = lambda E, O=out_ap, I=i: E.dma_start(out=O, in_=I, accum_op=ALU.add)
        else:
            fn = lambda E, O=out_ap, I=i: E.dma_start(out=O, in_=I)
        return self.op(q, fn, ins=(inb,), outs=outs,
                       inc="d0" if q == "sync" else "d1", amt=16)

    def memset(self, outb, o, val, eng="gpsimd", inc="po"):
        return self.op(eng, lambda E, O=o, v=val: E.memset(O, v),
                       outs=(outb,), inc=inc)


class Arena:

    def __init__(self, t, n):
        self.t, self.n = t, n
        self.free = [(0, n, [])]
        self.live = {}

    def alloc(self, name, cols, dtype=BF16):
        need = cols * (2 if dtype == F32 else 1)
        need = (need + 1) // 2 * 2
        for i, (a, sz, toks) in enumerate(self.free):
            if sz >= need:
                if sz == need:
                    del self.free[i]
                else:
                    self.free[i] = (a + need, sz - need, list(toks))
                for nm2, (a2, s2, _) in self.live.items():
                    assert a + need <= a2 or a2 + s2 <= a, \
                        f"arena overlap: {name} vs {nm2}"
                self.live[name] = (a, need, None)
                ap = self.t[:, a:a + need]
                b = Buf(ap.bitcast(F32) if dtype == F32 else ap, name)
                b.rd = [t_ for t_ in toks if t_ is not None]
                self.live[name] = (a, need, b)
                return b
        raise MemoryError(f"arena OOM: {name} needs {need} free={self.free} "
                          f"live={sorted(self.live)}")

    def freebuf(self, *names):
        for name in names:
            a, sz, b = self.live.pop(name)
            toks = list(b.rd)
            if b.wr is not None:
                toks.append(b.wr)
            self.free.append((a, sz, toks))
        self.free.sort()
        merged = []
        for a2, s2, t2 in self.free:
            if merged and merged[-1][0] + merged[-1][1] == a2:
                merged[-1][1] += s2
                merged[-1][2] = merged[-1][2] + t2
            else:
                merged.append([a2, s2, list(t2)])
        self.free = [(m[0], m[1], m[2]) for m in merged]


def rearr_w(w_ap):
    return w_ap.rearrange("(c p) j -> p c j", p=128)


def build_nc(phase_limit=PHASE_LIMIT, dbg=DBG):
    nc = bass.Bass("TRN2", target_bir_lowering=False)
    S = Sched()
    C = Ctx(S)
    es = ExitStack()

    def din(name, shape, dtype=F32):
        return nc.dram_tensor(name, list(shape), dtype, kind="ExternalInput").ap()

    def dscr(name, shape, dtype=F32):
        kind = "ExternalOutput" if dbg else "Internal"
        return Buf(nc.dram_tensor(name, list(shape), dtype, kind=kind).ap(), name)

    IN = {}
    for nm, shp, dt in [
        ("xb", (T, D), F32), ("x_own", (TL, D), F32),
        ("Wq", (D, D), F32), ("Wk", (D, D), F32), ("Wv", (D, 2 * D), F32),
        ("Wg", (D, 2 * D), F32), ("Wo", (2 * D, D), F32),
        ("wiT", (D, 3 * D), F32), ("woT", (D, D), F32),
        ("w1", (E, D, FF), F32), ("w2", (E, FF, D), F32),
        ("wgrp", (D, 4), F32), ("wexp", (D, E), F32),
        ("ln_w", (1, D), F32), ("ln_b", (1, D), F32), ("gnw", (1, DV), F32),
        ("bi", (1, 3 * D), F32), ("bo_v", (1, D), F32),
        ("l1g", (1, D), F32), ("l1b", (1, D), F32),
        ("l2g", (1, D), F32), ("l2b", (1, D), F32),
        ("bgrp", (4, 1), F32), ("bexp", (E, 1), F32),
        ("b1", (E, FF), F32), ("b2", (E, D), F32),
        ("cos_tok", (T, 128), BF16), ("sin_tok", (T, 128), BF16),
        ("cos_own", (TL, 128), BF16), ("sin_own", (TL, 128), BF16),
        ("cos_f", (128, TL), BF16), ("sin_f", (128, TL), BF16),
        ("dmaskT", (H * 2 * 128, CP), F32), ("crossrep", (128, H * CP), BF16),
        ("kdec_c", (128, H * 2), F32), ("pfx_c", (128, NPFX * H * 2), F32),
        ("ident_b", (128, 128), BF16), ("ident_f", (128, 128), F32),
        ("ones_row_b", (1, 512), BF16), ("ones_row_f", (1, 128), F32),
        ("ones_col_b", (128, 1), BF16), ("pairsel", (128, 2), BF16),
        ("selrow8", (8, 8 * 128), BF16), ("zeros_col", (128, 1), F32),
    ]:
        IN[nm] = din(nm, shp, dt)

    out_d = nc.dram_tensor("out", [TL, D], F32, kind="ExternalOutput").ap()
    out_b = Buf(out_d, "out")
    krtok_d = dscr("krtok_d", (PFX_TOK, D), BF16)
    vrtok_d = dscr("vrtok_d", (PFX_TOK, 2 * D), BF16)
    mkT_d = dscr("mkT_d", (D, T), BF16)
    mv_d = dscr("mv_d", (T, D), BF16)
    qrT_d = dscr("qrT_d", (D, TL), BF16)
    gT_d = dscr("gT_d", (2 * D, TL), BF16)
    krown_d = dscr("krown_d", (TL, D), BF16)
    vrown_d = dscr("vrown_d", (TL, 2 * D), BF16)
    qmT_d = dscr("qmT_d", (D, TL), BF16)
    ow_d = dscr("ow_d", (2 * D, TL), BF16)
    retout_d = dscr("retout_d", (TL, D), F32)
    y_d = dscr("y_d", (TL, D), F32)
    h1_d = dscr("h1_d", (TL, D), F32)
    row_d = dscr("row_d", (4, 4096), F32)
    dbgd = {}
    if dbg:
        for nm in ["dbg_a", "dbg_b", "dbg_c"]:
            dbgd[nm] = Buf(nc.dram_tensor(nm, [128, 4096], F32,
                                          kind="ExternalOutput").ap(), nm)

    def sbuf(name, shape, dtype):
        h = es.enter_context(nc.sbuf_tensor(name, list(shape), dtype))
        return Buf(h[:, :], name)

    NAR = 79872
    ar_h = es.enter_context(nc.sbuf_tensor("arena", [128, NAR], BF16))
    A = Arena(ar_h[:, :], NAR)

    idb = sbuf("idb", (128, 128), BF16)
    idf = sbuf("idf", (128, 128), F32)
    ones_b = sbuf("ones_b", (1, 512), BF16)
    ones_f = sbuf("ones_f", (1, 128), F32)
    onescol = sbuf("onescol", (128, 1), BF16)
    zcol = sbuf("zcol", (128, 1), F32)
    psel = sbuf("psel", (128, 2), BF16)
    kdec_sb = sbuf("kdec_sb", (128, H * 2), F32)
    pfx_sb = sbuf("pfx_sb", (128, NPFX * H * 2), F32)
    gnw_sb = sbuf("gnw_sb", (128, 4), F32)
    lnw_sb = sbuf("lnw_sb", (128, 8), F32)
    lnbc = sbuf("lnbc", (128, 8), BF16)
    lnbc_f = sbuf("lnbc_f", (128, 8), F32)
    stc = sbuf("stc", (128, 40), F32)
    mu1o = sbuf("mu1o", (128, 16), F32)
    rs1o = sbuf("rs1o", (128, 16), F32)
    maxk_sb = sbuf("maxk_sb", (2, 128), F32)
    maxq_sb = sbuf("maxq_sb", (2, 32), F32)
    mh_bias = sbuf("mh_bias", (128, NH + 8), F32)
    brq = sbuf("brq", (128, 8), F32)
    brg = sbuf("brg", (128, 16), F32)
    bmq = sbuf("bmq", (128, 8), F32)
    bmk = sbuf("bmk", (128, 8), F32)
    brk_rep = sbuf("brk_rep", (128, D), BF16)
    brv_rep = sbuf("brv_rep", (128, 2 * D), BF16)
    bmv_rep = sbuf("bmv_rep", (128, D), BF16)
    vec8 = sbuf("vec8", (16, 128), F32)
    row3 = sbuf("row3", (1, TL), F32)
    drow = sbuf("drow", (1, TL), F32)
    dstage = sbuf("dstage", (1, TL), F32)
    dstage2 = sbuf("dstage2", (1, TL), F32)
    b1c = sbuf("b1c", (128, E * 16), F32)
    b2_sb = sbuf("b2_sb", (8, D), BF16)
    bgrp_sb = sbuf("bgrp_sb", (4, 1), F32)
    bexp_sb = sbuf("bexp_sb", (8, 1), F32)
    sel8_sb = sbuf("sel8_sb", (8, 8 * 128), BF16)
    wgrp_sb = sbuf("wgrp_sb", (128, 8 * 4), BF16)
    wexp_sb = sbuf("wexp_sb", (128, 8 * 8), BF16)

    def psum(name, shape, dtype=F32):
        h = es.enter_context(nc.psum_tensor(name, list(shape), dtype))
        return h[:, :]

    PSA = psum("PSA", (128, 2048))
    PSB = psum("PSB", (128, 1024))
    PSC = psum("PSC", (128, 512))
    PST = psum("PST", (128, 1024), BF16)
    psA = [Buf(PSA[:, i * 512:(i + 1) * 512], f"psA{i}") for i in range(4)]
    psAa = Buf(PSA[:, 0:1024], "psAa")
    psAb = Buf(PSA[:, 1024:2048], "psAb")
    psAw = Buf(PSA[:, :], "psAw")
    psB = [Buf(PSB[:, 0:512], "psB0"), Buf(PSB[:, 512:1024], "psB1")]
    psBw = Buf(PSB[:, :], "psBw")
    psC = Buf(PSC[:, :], "psC")
    psC2 = Buf(PSC[64:66, 0:512], "psC2")
    psT = Buf(PST[:, :], "psT")

    def alias(*bufs):
        deps = []
        for b in bufs:
            deps.append(b.wr)
            deps.extend(b.rd)
        return [d for d in deps if d is not None]

    sems = {}
    for s_ in ["d0", "d1", "pe", "ve", "ac", "po"]:
        sems[s_] = es.enter_context(nc.semaphore(s_))
        S._reg(s_)

    C.load(idb, idb.ap, IN["ident_b"])
    C.load(idf, idf.ap, IN["ident_f"])
    C.load(ones_b, ones_b.ap, IN["ones_row_b"])
    C.load(ones_f, ones_f.ap, IN["ones_row_f"])
    C.load(onescol, onescol.ap, IN["ones_col_b"])
    C.load(zcol, zcol.ap, IN["zeros_col"])
    C.load(psel, psel.ap, IN["pairsel"])
    C.load(kdec_sb, kdec_sb.ap, IN["kdec_c"])
    C.load(pfx_sb, pfx_sb.ap, IN["pfx_c"])
    C.load(sel8_sb, sel8_sb.ap, IN["selrow8"])
    C.load(bgrp_sb, bgrp_sb.ap, IN["bgrp"])
    C.load(bexp_sb, bexp_sb.ap, IN["bexp"])
    C.load(b2_sb, b2_sb.ap, IN["b2"], q="gpsimd")
    C.load(wgrp_sb, wgrp_sb.ap.rearrange("p (c j) -> p c j", c=8),
           rearr_w(IN["wgrp"]), q="gpsimd")
    C.load(wexp_sb, wexp_sb.ap.rearrange("p (c j) -> p c j", c=8),
           rearr_w(IN["wexp"]), q="gpsimd")

    def vec_to_cols(dstb, src_row, n):
        rows = n // 128
        C.load(vec8, vec8.ap[0:rows, 0:128],
               src_row.rearrange("o (r p) -> (o r) p", p=128))
        C.transp(psC, PSC[0:128, 0:rows], vec8.ap[0:rows, 0:128],
                 idf.ap[0:rows, 0:rows], ins=(vec8, idf))
        C.CPY(dstb, dstb.ap, psC, PSC[0:128, 0:rows])

    vec_to_cols(gnw_sb, IN["gnw"], 512)
    vec_to_cols(lnw_sb, IN["ln_w"], 1024)
    vec_to_cols(lnbc_f, IN["ln_b"], 1024)
    C.CPY(lnbc, lnbc.ap, lnbc_f, lnbc_f.ap)
    vec_to_cols(bmk, IN["bi"][0:1, D:2 * D], 1024)
    for e_ in range(E):
        rows = FF // 128
        C.load(vec8, vec8.ap[0:rows, 0:128],
               IN["b1"][e_:e_ + 1, :].rearrange("o (r p) -> (o r) p", p=128))
        C.transp(psC, PSC[0:128, 0:rows], vec8.ap[0:rows, 0:128],
                 idf.ap[0:rows, 0:rows], ins=(vec8, idf))
        C.CPY(b1c, b1c.ap[:, e_ * 16:(e_ + 1) * 16], psC, PSC[0:128, 0:rows])

    def rep_row_from_dram(dstb, src_dram, n, scale=None):
        C.load(row3, row3.ap[0:1, 0:n], src_dram)
        if scale is not None:
            C.TSi(row3, row3.ap[0:1, 0:n], row3, row3.ap[0:1, 0:n], scale)
        for j in range(0, n, 512):
            w = min(512, n - j)
            C.mm_group(psC, PSC[:, 0:w],
                       [(ones_f.ap[0:1, 0:128], row3.ap[0:1, j:j + w])],
                       ins=(ones_f, row3))
            C.CPY(dstb, dstb.ap[:, j:j + w], psC, PSC[:, 0:w])

    rep_row_from_dram(bmv_rep, IN["bi"][0:1, 2 * D:3 * D], D)

    if phase_limit >= 1:
        wk = A.alloc("wk", 8 * D)
        wv = A.alloc("wv", 8 * 2 * D)
        wik = A.alloc("wik", 8 * D)
        wiv = A.alloc("wiv", 8 * D)
        r8 = lambda b_: b_.ap.rearrange("p (c j) -> p c j", c=8)
        C.load(wk, r8(wk), rearr_w(IN["Wk"]), q="gpsimd")
        C.load(wv, r8(wv), rearr_w(IN["Wv"]), q="gpsimd")
        C.load(wik, r8(wik), rearr_w(IN["wiT"][:, D:2 * D]), q="gpsimd")
        C.load(wiv, r8(wiv), rearr_w(IN["wiT"][:, 2 * D:3 * D]), q="gpsimd")

        def bias_rows(w_b, ncols, repb):
            for j in range(0, ncols, 512):
                pairs = [(lnbc.ap[:, c:c + 1],
                          w_b.ap[:, c * ncols + j:c * ncols + j + 512])
                         for c in range(8)]
                C.mm_group(psB[0], PSB[0:1, 0:512], pairs, ins=(lnbc, w_b))
                C.CPY(row3, row3.ap[0:1, 0:512], psB[0], PSB[0:1, 0:512])
                C.mm_group(psC, PSC[:, 0:512],
                           [(ones_f.ap[0:1, 0:128], row3.ap[0:1, 0:512])],
                           ins=(ones_f, row3))
                C.CPY(repb, repb.ap[:, j:j + 512], psC, PSC[:, 0:512])

        bias_rows(wk, D, brk_rep)
        bias_rows(wv, 2 * D, brv_rep)
        for c in range(8):
            C.TS(wk, wk.ap[:, c * D:(c + 1) * D], wk, wk.ap[:, c * D:(c + 1) * D],
                 lnw_sb, lnw_sb.ap[:, c:c + 1])
        for c in range(8):
            C.TS(wv, wv.ap[:, c * 2 * D:(c + 1) * 2 * D],
                 wv, wv.ap[:, c * 2 * D:(c + 1) * 2 * D],
                 lnw_sb, lnw_sb.ap[:, c:c + 1])

        xf = [A.alloc("xf0", 1024, F32), A.alloc("xf1", 1024, F32)]
        sqs = A.alloc("sqs", 1024)
        xnf = [A.alloc("xnf0", 1024, F32), A.alloc("xnf1", 1024, F32)]
        ztile = [A.alloc("ztile0", 1024), A.alloc("ztile1", 1024)]
        xnb = [A.alloc("xnb0", 1024), A.alloc("xnb1", 1024)]
        _z0 = A.alloc("zTst0", 4096)
        _x0 = A.alloc("xnTst0", 4096)
        _z1 = A.alloc("zTst1", 4096)
        _x1 = A.alloc("xnTst1", 4096)
        zTst = [_z0, _z1]
        xnTst = [_x0, _x1]
        ctb = [A.alloc("ct0", 128), A.alloc("ct1", 128)]
        stb = [A.alloc("st0", 128), A.alloc("st1", 128)]
        krst = A.alloc("krst", 1024)
        krop = A.alloc("krop", 1024)
        rs1 = A.alloc("rs1", 512)
        rs2 = A.alloc("rs2", 512)
        vrst2 = [A.alloc("vrst", 512), A.alloc("vrstb", 512)]
        mkst2 = [A.alloc("mkst0", 512), A.alloc("mkst1", 512)]
        ksq2 = [A.alloc("ksq", 512), A.alloc("ksqb", 512)]
        mvst2 = [A.alloc("mvst0", 1024), A.alloc("mvst1", 1024)]

        stc2b = Buf(stc.ap[:, 24:40], "stc2b")
        stcP = [Buf(stc.ap[:, 0:16], "stcP0"), stc2b]
        def ln_pair(xbuf, gi, store_own):
            par = gi % 2
            c = stcP[0] if par == 0 else stcP[1]
            xnfp, ztp, xbp = xnf[par], ztile[par], xnb[par]
            C.ACT(sqs, sqs.ap, xbuf, xbuf.ap, AF.Identity, accb=c,
                  acc=c.ap[:, 0:1])
            C.ACT(sqs, sqs.ap, xbuf, xbuf.ap, AF.Square, accb=c, acc=c.ap[:, 1:2])
            C.TSi(c, c.ap[:, 2:3], c, c.ap[:, 0:1], 1.0 / D)
            C.TT(c, c.ap[:, 3:4], c, c.ap[:, 2:3], c, c.ap[:, 2:3])
            C.TSi(c, c.ap[:, 4:5], c, c.ap[:, 1:2], 1.0 / D, EPS, ALU.mult, ALU.add)
            C.TT(c, c.ap[:, 5:6], c, c.ap[:, 4:5], c, c.ap[:, 3:4], ALU.subtract)
            C.ACT(c, c.ap[:, 6:7], c, c.ap[:, 5:6], AF.Sqrt)
            C.RECIP(c, c.ap[:, 7:8], c, c.ap[:, 6:7])
            C.TS(xnfp, xnfp.ap, xbuf, xbuf.ap, c, c.ap[:, 2:3], c, c.ap[:, 7:8],
                 ALU.subtract, ALU.mult)
            C.ACT(xbp, xbp.ap, xnfp, xnfp.ap, AF.Identity, accb=c,
                  acc=c.ap[:, 8:9])
            if store_own:
                C.CPY(mu1o, mu1o.ap[:, gi:gi + 1], c, c.ap[:, 2:3])
                C.CPY(rs1o, rs1o.ap[:, gi:gi + 1], c, c.ap[:, 7:8])
            C.ACT(sqs, sqs.ap, xnfp, xnfp.ap, AF.Square, accb=c, acc=c.ap[:, 9:10])
            C.TSi(c, c.ap[:, 10:11], c, c.ap[:, 8:9], 1.0 / D)
            C.TT(c, c.ap[:, 11:12], c, c.ap[:, 10:11], c, c.ap[:, 10:11])
            C.TSi(c, c.ap[:, 12:13], c, c.ap[:, 9:10], 1.0 / D, EPS, ALU.mult, ALU.add)
            C.TT(c, c.ap[:, 13:14], c, c.ap[:, 12:13], c, c.ap[:, 11:12], ALU.subtract)
            C.ACT(c, c.ap[:, 14:15], c, c.ap[:, 13:14], AF.Sqrt)
            C.RECIP(c, c.ap[:, 15:16], c, c.ap[:, 14:15])
            C.TS(ztp, ztp.ap, xnfp, xnfp.ap, c, c.ap[:, 10:11], c, c.ap[:, 15:16],
                 ALU.subtract, ALU.mult)

        tp_par = [0]

        def transpose_into(srcb, dstb, tt, colsz=512):
            for half in range(2):
                hb = psB[tp_par[0]]
                off = tp_par[0] * 512
                tp_par[0] ^= 1
                bv = PSB[:, off:off + 512].bitcast(BF16)
                for c4 in range(4):
                    c = half * 4 + c4
                    C.transp(hb, bv[:, c4 * 128:(c4 + 1) * 128],
                             srcb.ap[:, c * 128:(c + 1) * 128], idb.ap,
                             ins=(srcb, idb), first=(c4 == 0), last=(c4 == 3))
                dst = dstb.ap.rearrange("p (c t) -> p c t", t=colsz)[
                    :, half * 4:half * 4 + 4, tt * 128:(tt + 1) * 128]
                src = bv[:, 0:512].rearrange("p (c t) -> p c t", t=128)
                C.op("vector", lambda E, O=dst, I=src: E.tensor_copy(O, I),
                     ins=(hb,), outs=(dstb,), inc="ve")

        def rope6(outb, o1, o2, inb, i1, i2, cb, c_ap, sbuf_, s_ap):
            C.TT(rs1, rs1.ap, inb, i1, cb, c_ap)
            C.TT(rs2, rs2.ap, inb, i2, sbuf_, s_ap)
            C.TT(outb, o1, rs1, rs1.ap, rs2, rs2.ap, ALU.subtract)
            C.TT(rs1, rs1.ap, inb, i2, cb, c_ap)
            C.TT(rs2, rs2.ap, inb, i1, sbuf_, s_ap)
            C.TT(outb, o2, rs1, rs1.ap, rs2, rs2.ap, ALU.add)

        def proj_tok(zstageb, wbuf, ncols, tt, psget, biasrep, outb, outsl):
            for half in range(ncols // 512):
                pb, ps = psget(half)
                pairs = [(zstageb.ap.rearrange("p (c t) -> p c t", t=512)
                          [:, c, tt * 128:(tt + 1) * 128],
                          wbuf.ap[:, c * ncols + half * 512:c * ncols + half * 512 + 512])
                         for c in range(8)]
                pairs.append((ones_b.ap[0:1, 0:128],
                              biasrep.ap[0:1, half * 512:half * 512 + 512]))
                C.mm_group(pb, ps, pairs, ins=(zstageb, wbuf, biasrep))
                C.ACT(outb, outsl[:, half * 512:(half + 1) * 512], pb, ps,
                      AF.Identity)

        psrot = [psA[0], psA[1], psA[2], psA[3]]

        def emit_ln(seg, tt):
            gi = seg * 4 + tt
            xbuf = xf[gi % 2]
            C.load(xbuf, xbuf.ap, IN["xb"][gi * 128:(gi + 1) * 128, :])
            ln_pair(xbuf, gi, False)

        def emit_tr(seg, tt):
            gi = seg * 4 + tt
            transpose_into(ztile[gi % 2], zTst[seg % 2], tt)
            transpose_into(xnb[gi % 2], xnTst[seg % 2], tt)

        for tt in range(4):
            emit_ln(0, tt)
            emit_tr(0, tt)
        for seg in range(16):
            sb_z, sb_x = zTst[seg % 2], xnTst[seg % 2]
            do_ret = seg * 512 < PFX_TOK
            for tt in range(4):
                if seg + 1 < 16:
                    emit_ln(seg + 1, tt)
                gi = seg * 4 + tt
                if do_ret:
                    C.load(ctb[tt % 2], ctb[tt % 2].ap,
                           IN["cos_tok"][gi * 128:(gi + 1) * 128, :])
                    C.load(stb[tt % 2], stb[tt % 2].ap,
                           IN["sin_tok"][gi * 128:(gi + 1) * 128, :])
                    proj_tok(sb_z, wk, D, tt,
                             lambda h: (psrot[h], psrot[h].ap), brk_rep,
                             krst, krst.ap)
                    kv = krst.ap.rearrange("p (h u) -> p h u", u=256)
                    ko = krop.ap.rearrange("p (h u) -> p h u", u=256)
                    cb_ap = ctb[tt % 2].ap.unsqueeze(1).broadcast_to([128, 4, 128])
                    sb_ap = stb[tt % 2].ap.unsqueeze(1).broadcast_to([128, 4, 128])
                    rope6(krop, ko[:, :, 0:128], ko[:, :, 128:256],
                          krst, kv[:, :, 0:128], kv[:, :, 128:256],
                          ctb[tt % 2], cb_ap, stb[tt % 2], sb_ap)
                    C.stor(krtok_d.ap[gi * 128:(gi + 1) * 128, :], krop, krop.ap,
                           q="gpsimd", dstb=krtok_d)
                    for vh in range(4):
                        pb, ps = psrot[vh % 2 + 2], psrot[vh % 2 + 2].ap
                        pairs = [(sb_z.ap.rearrange("p (c t) -> p c t", t=512)
                                  [:, c, tt * 128:(tt + 1) * 128],
                                  wv.ap[:, c * 2 * D + vh * 512:
                                        c * 2 * D + vh * 512 + 512])
                                 for c in range(8)]
                        pairs.append((ones_b.ap[0:1, 0:128],
                                      brv_rep.ap[0:1, vh * 512:vh * 512 + 512]))
                        C.mm_group(pb, ps, pairs, ins=(sb_z, wv, brv_rep))
                        vrp = vrst2[vh % 2]
                        C.ACT(vrp, vrp.ap, pb, ps, AF.Identity)
                        C.stor(vrtok_d.ap[gi * 128:(gi + 1) * 128,
                                          vh * 512:(vh + 1) * 512],
                               vrp, vrp.ap, q="gpsimd", dstb=vrtok_d)
                mvp = mvst2[tt % 2]
                for half in range(2):
                    pb, ps = psrot[half], psrot[half].ap
                    pairs = [(sb_x.ap.rearrange("p (c t) -> p c t", t=512)
                              [:, c, tt * 128:(tt + 1) * 128],
                              wiv.ap[:, c * D + half * 512:c * D + half * 512 + 512])
                             for c in range(8)]
                    pairs.append((ones_b.ap[0:1, 0:128],
                                  bmv_rep.ap[0:1, half * 512:half * 512 + 512]))
                    C.mm_group(pb, ps, pairs, ins=(sb_x, wiv, bmv_rep))
                    C.ACT(mvp, mvp.ap[:, half * 512:(half + 1) * 512], pb, ps,
                          AF.Identity)
                C.stor(mv_d.ap[gi * 128:(gi + 1) * 128, :], mvp, mvp.ap,
                       q="gpsimd", dstb=mv_d)
                if seg + 1 < 16:
                    emit_tr(seg + 1, tt)
            if dbg and seg == 0:
                C.stor(dbgd["dbg_a"].ap, sb_x, sb_x.ap, q="gpsimd",
                       dstb=dbgd["dbg_a"])
                C.stor(dbgd["dbg_b"].ap, sb_z, sb_z.ap, q="gpsimd",
                       dstb=dbgd["dbg_b"])
            def norm_step(kc):
                kq = ksq2[kc % 2]
                if kc % 2 == 0:
                    pc, pcs, tp = psC, PSC[0:2, 0:512], (0, 0)
                else:
                    pc, pcs, tp = psC2, PSC[64:66, 0:512], (0, 64)
                C.op("tensor", lambda E, P=pcs, L=psel.ap, R=kq.ap, t=tp:
                     E.matmul(P, L, R, start=True, stop=True, tile_position=t),
                     ins=(psel, kq), outs=(pc,), inc="pe")
                C.RED(maxk_sb, maxk_sb.ap[:, kc * 16 + seg:kc * 16 + seg + 1],
                      pc, pcs)
            for kc in range(8):
                pb, ps = psrot[kc % 4], psrot[kc % 4].ap
                pairs = [(wik.ap[:, c * D + kc * 128:c * D + kc * 128 + 128],
                          sb_x.ap[:, c * 512:(c + 1) * 512]) for c in range(8)]
                C.mm_group(pb, ps, pairs, ins=(wik, sb_x))
                mkp = mkst2[kc % 2]
                C.ACT(mkp, mkp.ap, pb, ps, AF.Identity,
                      biasb=bmk, bias=bmk.ap[:, kc:kc + 1])
                C.stor(mkT_d.ap[kc * 128:(kc + 1) * 128, seg * 512:(seg + 1) * 512],
                       mkp, mkp.ap, q="gpsimd", dstb=mkT_d)
                C.ACT(ksq2[kc % 2], ksq2[kc % 2].ap, mkp, mkp.ap, AF.Square)
                if kc > 0:
                    norm_step(kc - 1)
            norm_step(7)

        A.freebuf("krst", "krop", "vrst", "vrstb", "mkst0", "mkst1",
                  "ksq", "ksqb",
                  "mvst0", "mvst1",
                  "wk", "wv", "wik", "wiv", "zTst0", "xnTst0", "zTst1", "xnTst1")

        zT_own = A.alloc("zT_own", 8 * TL)
        xnT_own = A.alloc("xnT_own", 8 * TL)
        def emit_ln15(gi):
            xbuf = xf[gi % 2]
            C.load(xbuf, xbuf.ap, IN["x_own"][gi * 128:(gi + 1) * 128, :])
            ln_pair(xbuf, gi, True)
        emit_ln15(0)
        for gi in range(16):
            if gi + 1 < 16:
                emit_ln15(gi + 1)
            transpose_into(ztile[gi % 2], zT_own, gi, colsz=TL)
            transpose_into(xnb[gi % 2], xnT_own, gi, colsz=TL)
        A.freebuf("xf0", "xf1", "sqs", "xnf0", "xnf1", "ztile0", "ztile1",
                  "xnb0", "xnb1", "ct0", "ct1", "st0", "st1",
                  "rs1", "rs2")


    if phase_limit >= 2:
        r8 = lambda b_, n_: b_.ap.rearrange("p (c j) -> p c j", c=8)

        def bias_cols(w_b, ncols, dstb, njc):
            for jc in range(njc):
                pairs = [(w_b.ap[:, c * ncols + jc * 128:c * ncols + jc * 128 + 128],
                          lnbc.ap[:, c:c + 1]) for c in range(8)]
                C.mm_group(psC, PSC[:, 0:1], pairs, ins=(w_b, lnbc))
                C.CPY(dstb, dstb.ap[:, jc:jc + 1], psC, PSC[:, 0:1])

        wg = A.alloc("wg", 8 * 2 * D)
        C.load(wg, r8(wg, 2 * D), rearr_w(IN["Wg"]), q="gpsimd")
        bias_cols(wg, 2 * D, brg, 16)
        for c in range(8):
            C.TS(wg, wg.ap[:, c * 2 * D:(c + 1) * 2 * D],
                 wg, wg.ap[:, c * 2 * D:(c + 1) * 2 * D],
                 lnw_sb, lnw_sb.ap[:, c:c + 1])
        gst = A.alloc("gst", TL)
        for gc in range(16):
            for ts4 in range(4):
                pb = psA[ts4]
                pairs = [(wg.ap[:, c * 2 * D + gc * 128:c * 2 * D + gc * 128 + 128],
                          zT_own.ap[:, c * TL + ts4 * 512:c * TL + ts4 * 512 + 512])
                         for c in range(8)]
                C.mm_group(pb, pb.ap, pairs, ins=(wg, zT_own))
                C.ACT(gst, gst.ap[:, ts4 * 512:(ts4 + 1) * 512],
                      pb, pb.ap, AF.Identity, biasb=brg, bias=brg.ap[:, gc:gc + 1])
            C.stor(gT_d.ap[gc * 128:(gc + 1) * 128, :], gst, gst.ap, q="gpsimd",
                   dstb=gT_d)
        A.freebuf("wg", "gst")

        wq = A.alloc("wq", 8 * D)
        C.load(wq, r8(wq, D), rearr_w(IN["Wq"]), q="gpsimd")
        bias_cols(wq, D, brq, 8)
        C.TSi(brq, brq.ap, brq, brq.ap, 1.0 / 16.0)
        for c in range(8):
            C.TS(wq, wq.ap[:, c * D:(c + 1) * D], wq, wq.ap[:, c * D:(c + 1) * D],
                 lnw_sb, lnw_sb.ap[:, c:c + 1], None, 1.0 / 16.0, ALU.mult, ALU.mult)
        qrT = A.alloc("qrT", 8 * TL)
        rf = [A.alloc(f"rf{i}", TL) for i in range(4)]
        cosf = A.alloc("cosf", TL)
        sinf = A.alloc("sinf", TL)
        C.load(cosf, cosf.ap, IN["cos_f"])
        C.load(sinf, sinf.ap, IN["sin_f"])
        for qc in range(8):
            for ts4 in range(4):
                pb = psA[ts4]
                pairs = [(wq.ap[:, c * D + qc * 128:c * D + qc * 128 + 128],
                          zT_own.ap[:, c * TL + ts4 * 512:c * TL + ts4 * 512 + 512])
                         for c in range(8)]
                C.mm_group(pb, pb.ap, pairs, ins=(wq, zT_own))
                C.ACT(qrT, qrT.ap[:, qc * TL + ts4 * 512:qc * TL + ts4 * 512 + 512],
                      pb, pb.ap, AF.Identity, biasb=brq, bias=brq.ap[:, qc:qc + 1])
        for h in range(H):
            q1 = qrT.ap[:, (h * 2) * TL:(h * 2) * TL + TL]
            q2 = qrT.ap[:, (h * 2 + 1) * TL:(h * 2 + 1) * TL + TL]
            C.TT(rf[0], rf[0].ap, qrT, q1, cosf, cosf.ap)
            C.TT(rf[1], rf[1].ap, qrT, q2, sinf, sinf.ap)
            C.TT(rf[2], rf[2].ap, qrT, q2, cosf, cosf.ap)
            C.TT(rf[3], rf[3].ap, qrT, q1, sinf, sinf.ap)
            C.TT(qrT, q1, rf[0], rf[0].ap, rf[1], rf[1].ap, ALU.subtract)
            C.TT(qrT, q2, rf[2], rf[2].ap, rf[3], rf[3].ap, ALU.add)
        A.freebuf("wq", "rf0", "rf1", "rf2", "rf3", "cosf", "sinf")

        wk2 = A.alloc("wk2", 8 * D)
        wv2 = A.alloc("wv2", 8 * 2 * D)
        C.load(wk2, r8(wk2, D), rearr_w(IN["Wk"]), q="gpsimd")
        C.load(wv2, r8(wv2, 2 * D), rearr_w(IN["Wv"]), q="gpsimd")
        for c in range(8):
            C.TS(wk2, wk2.ap[:, c * D:(c + 1) * D], wk2, wk2.ap[:, c * D:(c + 1) * D],
                 lnw_sb, lnw_sb.ap[:, c:c + 1])
        for c in range(8):
            C.TS(wv2, wv2.ap[:, c * 2 * D:(c + 1) * 2 * D],
                 wv2, wv2.ap[:, c * 2 * D:(c + 1) * 2 * D],
                 lnw_sb, lnw_sb.ap[:, c:c + 1])
        if dbg:
            C.stor(dbgd["dbg_a"].ap, xnT_own, xnT_own.ap[:, 0:4096], q="gpsimd",
                   dstb=dbgd["dbg_a"])
        vec_to_cols(bmq, IN["bi"][0:1, 0:D], 1024)
        C.TSi(bmq, bmq.ap, bmq, bmq.ap, 0.125)
        if dbg:
            C.stor(dbgd["dbg_b"].ap, xnT_own, xnT_own.ap[:, 0:4096], q="gpsimd",
                   dstb=dbgd["dbg_b"])

        krst2 = A.alloc("krst2", 1024)
        krop2 = A.alloc("krop2", 1024)
        vrst2 = A.alloc("vrst2", 2048)
        rr1 = A.alloc("rr1", 512)
        rr2 = A.alloc("rr2", 512)
        ct2 = [A.alloc("ct2a", 128), A.alloc("ct2b", 128)]
        st2 = [A.alloc("st2a", 128), A.alloc("st2b", 128)]
        for gi in range(16):
            C.load(ct2[gi % 2], ct2[gi % 2].ap, IN["cos_own"][gi * 128:(gi + 1) * 128, :])
            C.load(st2[gi % 2], st2[gi % 2].ap, IN["sin_own"][gi * 128:(gi + 1) * 128, :])
            for half in range(2):
                pb = psA[half]
                pairs = [(zT_own.ap[:, c * TL + gi * 128:c * TL + gi * 128 + 128],
                          wk2.ap[:, c * D + half * 512:c * D + half * 512 + 512])
                         for c in range(8)]
                C.mm_group(pb, pb.ap, pairs, ins=(zT_own, wk2))
                C.TT(krst2, krst2.ap[:, half * 512:(half + 1) * 512], pb, pb.ap,
                     brk_rep, brk_rep.ap[:, half * 512:(half + 1) * 512], ALU.add)
            kv = krst2.ap.rearrange("p (h u) -> p h u", u=256)
            ko = krop2.ap.rearrange("p (h u) -> p h u", u=256)
            cb_ap = ct2[gi % 2].ap.unsqueeze(1).broadcast_to([128, 4, 128])
            sb_ap = st2[gi % 2].ap.unsqueeze(1).broadcast_to([128, 4, 128])
            C.TT(rr1, rr1.ap.rearrange("p (h u) -> p h u", u=128), krst2,
                 kv[:, :, 0:128], ct2[gi % 2], cb_ap)
            C.TT(rr2, rr2.ap.rearrange("p (h u) -> p h u", u=128), krst2,
                 kv[:, :, 128:256], st2[gi % 2], sb_ap)
            C.TT(krop2, ko[:, :, 0:128], rr1,
                 rr1.ap.rearrange("p (h u) -> p h u", u=128), rr2,
                 rr2.ap.rearrange("p (h u) -> p h u", u=128), ALU.subtract)
            C.TT(rr1, rr1.ap.rearrange("p (h u) -> p h u", u=128), krst2,
                 kv[:, :, 128:256], ct2[gi % 2], cb_ap)
            C.TT(rr2, rr2.ap.rearrange("p (h u) -> p h u", u=128), krst2,
                 kv[:, :, 0:128], st2[gi % 2], sb_ap)
            C.TT(krop2, ko[:, :, 128:256], rr1,
                 rr1.ap.rearrange("p (h u) -> p h u", u=128), rr2,
                 rr2.ap.rearrange("p (h u) -> p h u", u=128), ALU.add)
            C.stor(krown_d.ap[gi * 128:(gi + 1) * 128, :], krop2, krop2.ap,
                   q="gpsimd", dstb=krown_d)
            for half in range(4):
                pb = psA[half]
                pairs = [(zT_own.ap[:, c * TL + gi * 128:c * TL + gi * 128 + 128],
                          wv2.ap[:, c * 2 * D + half * 512:c * 2 * D + half * 512 + 512])
                         for c in range(8)]
                C.mm_group(pb, pb.ap, pairs, ins=(zT_own, wv2))
                C.TT(vrst2, vrst2.ap[:, half * 512:(half + 1) * 512], pb, pb.ap,
                     brv_rep, brv_rep.ap[:, half * 512:(half + 1) * 512], ALU.add)
            C.stor(vrown_d.ap[gi * 128:(gi + 1) * 128, :], vrst2, vrst2.ap,
                   q="gpsimd", dstb=vrown_d)
        if dbg:
            C.stor(dbgd["dbg_c"].ap, xnT_own, xnT_own.ap[:, 0:4096], q="gpsimd",
                   dstb=dbgd["dbg_c"])
        A.freebuf("wk2", "wv2", "krst2", "krop2", "vrst2", "rr1", "rr2",
                  "ct2a", "ct2b", "st2a", "st2b", "zT_own")
        wiq = A.alloc("wiq", 8 * D)
        C.load(wiq, r8(wiq, D), rearr_w(IN["wiT"][:, 0:D]), q="gpsimd")
        C.TSi(wiq, wiq.ap, wiq, wiq.ap, 0.125)
        qmT = A.alloc("qmT", 8 * TL)
        qmsq = A.alloc("qmsq", 512)
        for qc in range(8):
            for ts4 in range(4):
                pb = psA[ts4]
                pairs = [(wiq.ap[:, c * D + qc * 128:c * D + qc * 128 + 128],
                          xnT_own.ap[:, c * TL + ts4 * 512:c * TL + ts4 * 512 + 512])
                         for c in range(8)]
                C.mm_group(pb, pb.ap, pairs, ins=(wiq, xnT_own))
                C.ACT(qmT, qmT.ap[:, qc * TL + ts4 * 512:qc * TL + ts4 * 512 + 512],
                      pb, pb.ap, AF.Identity, biasb=bmq, bias=bmq.ap[:, qc:qc + 1])
                C.ACT(qmsq, qmsq.ap, qmT,
                      qmT.ap[:, qc * TL + ts4 * 512:qc * TL + ts4 * 512 + 512],
                      AF.Square)
                C.mm_group(psC, PSC[0:2, 0:512], [(psel.ap, qmsq.ap)],
                           ins=(psel, qmsq))
                C.RED(maxq_sb, maxq_sb.ap[:, qc * 4 + ts4:qc * 4 + ts4 + 1],
                      psC, PSC[0:2, 0:512])
        for qc in range(8):
            C.stor(qmT_d.ap[qc * 128:(qc + 1) * 128, :], qmT,
                   qmT.ap[:, qc * TL:(qc + 1) * TL], q="gpsimd", dstb=qmT_d)
        A.freebuf("wiq", "qmsq", "xnT_own", "qmT")

        S_sb = A.alloc("S_sb", H * 2 * 512, F32)
        S_bf = A.alloc("S_bf", H * 2 * 512)
        krp = [A.alloc("krp0", 2048), A.alloc("krp1", 2048)]
        vrp = [A.alloc("vrp0", 4096), A.alloc("vrp1", 4096)]
        ksdp = A.alloc("ksdp", 512)
        for hp in range(2):
            started = [[False] * 2 for _ in range(2)]
            for m in range(NPFX):
                kb, vb = krp[m % 2], vrp[m % 2]
                C.load(kb, kb.ap.rearrange("p (a j) -> p a j", a=2),
                       IN_krp := krtok_d.ap[m * 256:(m + 1) * 256, :].rearrange(
                           "(a p) j -> p a j", p=128), srcb=krtok_d)
                C.load(vb, vb.ap.rearrange("p (a j) -> p a j", a=2),
                       vrtok_d.ap[m * 256:(m + 1) * 256, :].rearrange(
                           "(a p) j -> p a j", p=128), srcb=vrtok_d)
                for hh in range(2):
                    h = hp * 2 + hh
                    for kt in range(2):
                        C.TS(ksdp, ksdp.ap[:, hh * 256:(hh + 1) * 256], kb,
                             kb.ap[:, kt * 1024 + h * 256:kt * 1024 + h * 256 + 256],
                             pfx_sb, pfx_sb.ap[:, (m * H + h) * 2 + kt:
                                               (m * H + h) * 2 + kt + 1])
                        for c in range(2):
                            pslot = psA[hh * 2 + c]
                            st_ = not started[hh][c]
                            started[hh][c] = True
                            last = (m == NPFX - 1 and kt == 1)
                            C.op("tensor",
                                 lambda E, P=pslot.ap, L=ksdp.ap[
                                     :, hh * 256 + c * 128:hh * 256 + c * 128 + 128],
                                 R=vb.ap[:, kt * 2048 + h * 512:
                                         kt * 2048 + h * 512 + 512],
                                 a=st_, b=last: E.matmul(P, L, R, start=a, stop=b),
                                 ins=(ksdp, vb), outs=(pslot,) if last else (),
                                 inc="pe")
            for hh in range(2):
                h = hp * 2 + hh
                for c in range(2):
                    blkslice = slice((h * 2 + c) * 512, (h * 2 + c) * 512 + 512)
                    C.CPY(S_sb, S_sb.ap[:, blkslice], psA[hh * 2 + c],
                          psA[hh * 2 + c].ap)
        for h in range(H):
            C.CPY(S_bf, S_bf.ap[:, h * 1024:(h + 1) * 1024], S_sb,
                  S_sb.ap[:, h * 1024:(h + 1) * 1024])

        A.freebuf("krp0", "krp1", "vrp0", "vrp1", "ksdp")
        dmask_sb = A.alloc("dmask_sb", H * 2 * CP, F32)
        for r in range(8):
            C.load(dmask_sb, dmask_sb.ap[:, r * 256:(r + 1) * 256],
                   IN["dmaskT"][r * 128:(r + 1) * 128, :])
        crep = A.alloc("crep", H * CP)
        C.load(crep, crep.ap, IN["crossrep"])
        krT_sc = A.alloc("krT_sc", 2048)
        qs_sc = A.alloc("qs_sc", 2048)
        ksd_sc = A.alloc("ksd_sc", 512)
        pt_sc = A.alloc("pt_sc", 512)
        of32 = A.alloc("of32", 1024, F32)
        osq = A.alloc("osq", 1024)
        rrep_sb = A.alloc("rrep_sb", 256, F32)
        ont = A.alloc("ont", 1024)
        sil = A.alloc("sil", 1024)
        ow_sc = A.alloc("ow_sc", 4096)
        gsb = A.alloc("gsb", 4096)
        kro = [A.alloc("kro0", 2048), A.alloc("kro1", 2048)]
        vro = [A.alloc("vro0", 4096), A.alloc("vro1", 4096)]
        for sc in range(NSC):
            kb, vb = kro[sc % 2], vro[sc % 2]
            C.load(kb, kb.ap.rearrange("p (a j) -> p a j", a=2),
                   krown_d.ap[sc * 256:(sc + 1) * 256, :].rearrange(
                       "(a p) j -> p a j", p=128), srcb=krown_d)
            C.load(vb, vb.ap.rearrange("p (a j) -> p a j", a=2),
                   vrown_d.ap[sc * 256:(sc + 1) * 256, :].rearrange(
                       "(a p) j -> p a j", p=128), srcb=vrown_d)
            for gc in range(16):
                C.load(gsb, gsb.ap[:, gc * 256:(gc + 1) * 256],
                       gT_d.ap[gc * 128:(gc + 1) * 128, sc * 256:(sc + 1) * 256],
                       srcb=gT_d)
            for rnd in range(2):
                for i4 in range(4):
                    idx = rnd * 4 + i4
                    h, c = idx // 2, idx % 2
                    for kt in range(2):
                        C.transp(psT, PST[:, (i4 * 2 + kt) * 128:
                                          (i4 * 2 + kt) * 128 + 128],
                                 kb.ap[:, kt * 1024 + h * 256 + c * 128:
                                       kt * 1024 + h * 256 + c * 128 + 128],
                                 idb.ap, ins=(kb, idb),
                                 first=(i4 == 0 and kt == 0),
                                 last=(i4 == 3 and kt == 1))
                dst = krT_sc.ap[:, rnd * 1024:(rnd + 1) * 1024]
                C.op("vector", lambda E, O=dst, I=PST[:, 0:1024]:
                     E.tensor_copy(O, I), ins=(psT,), outs=(krT_sc,), inc="ve")
            for idx in range(8):
                h, c = idx // 2, idx % 2
                C.TT(qs_sc, qs_sc.ap[:, idx * 256:(idx + 1) * 256], qrT,
                     qrT.ap[:, idx * TL + sc * 256:idx * TL + sc * 256 + 256],
                     crep, crep.ap[:, h * CP:(h + 1) * CP])
            for h in range(H):
                for kt in range(2):
                    pairs = [(krT_sc.ap[:, (h * 2 + c) * 256 + kt * 128:
                                        (h * 2 + c) * 256 + kt * 128 + 128],
                              qrT.ap[:, (h * 2 + c) * TL + sc * 256:
                                     (h * 2 + c) * TL + sc * 256 + 256])
                             for c in range(2)]
                    C.mm_group(psB[kt], PSB[:, kt * 512:kt * 512 + 256], pairs,
                               ins=(krT_sc, qrT))
                    C.TT(pt_sc, pt_sc.ap[:, kt * 256:(kt + 1) * 256], psB[kt],
                         PSB[:, kt * 512:kt * 512 + 256], dmask_sb,
                         dmask_sb.ap[:, (h * 2 + kt) * 256:(h * 2 + kt) * 256 + 256])
                for dvc in range(4):
                    after = []
                    pslot_ap = PSA[:, dvc * 256:(dvc + 1) * 256]
                    psbuf = psA[dvc // 2]
                    seqs = []
                    for kt in range(2):
                        seqs.append((vb.ap[:, kt * 2048 + h * 512 + dvc * 128:
                                           kt * 2048 + h * 512 + dvc * 128 + 128],
                                     pt_sc.ap[:, kt * 256:(kt + 1) * 256]))
                    for c in range(2):
                        seqs.append((S_bf.ap[:, (h * 2 + c) * 512 + dvc * 128:
                                             (h * 2 + c) * 512 + dvc * 128 + 128],
                                     qs_sc.ap[:, (h * 2 + c) * 256:
                                              (h * 2 + c) * 256 + 256]))
                    n_ = len(seqs)
                    afterb = [vb.wr, pt_sc.wr, S_bf.wr, qs_sc.wr, psbuf.wr]
                    afterb.extend(psbuf.rd)
                    tok = None
                    for i_, (L, R) in enumerate(seqs):
                        tok = S.emit("tensor",
                                     lambda E, P=pslot_ap, LL=L, RR=R,
                                     a=(i_ == 0), b=(i_ == n_ - 1):
                                     E.matmul(P, LL, RR, start=a, stop=b),
                                     inc="pe" if i_ == n_ - 1 else None,
                                     after=afterb if i_ == 0 else ())
                    for b_ in (vb, pt_sc, S_bf, qs_sc):
                        b_.rd.append(tok)
                    psbuf.wr = tok
                    psbuf.rd = []
                for kt in range(2):
                    C.TS(ksd_sc, ksd_sc.ap[:, kt * 256:(kt + 1) * 256], kb,
                         kb.ap[:, kt * 1024 + h * 256:kt * 1024 + h * 256 + 256],
                         kdec_sb, kdec_sb.ap[:, h * 2 + kt:h * 2 + kt + 1])
                for c in range(2):
                    pairs = [(ksd_sc.ap[:, kt * 256 + c * 128:kt * 256 + c * 128 + 128],
                              vb.ap[:, kt * 2048 + h * 512:kt * 2048 + h * 512 + 512])
                             for kt in range(2)]
                    C.mm_group(psA[2 + c], PSA[:, (2 + c) * 512:(3 + c) * 512],
                               pairs, ins=(ksd_sc, vb))
                C.op("vector", lambda E, O=of32.ap, I=PSA[:, 0:1024]:
                     E.tensor_copy(O, I), ins=(psA[0], psA[1]), outs=(of32,),
                     inc="ve")
                for c in range(2):
                    C.STT(S_sb, S_sb.ap[:, (h * 2 + c) * 512:(h * 2 + c) * 512 + 512],
                          S_sb, S_sb.ap[:, (h * 2 + c) * 512:(h * 2 + c) * 512 + 512],
                          GC[h], psA[2 + c], PSA[:, (2 + c) * 512:(3 + c) * 512],
                          ALU.mult, ALU.add)
                C.CPY(S_bf, S_bf.ap[:, h * 1024:(h + 1) * 1024], S_sb,
                      S_sb.ap[:, h * 1024:(h + 1) * 1024])
                C.ACT(osq, osq.ap, of32, of32.ap, AF.Square)
                pairs = [(onescol.ap, osq.ap[:, dvc * 256:(dvc + 1) * 256])
                         for dvc in range(4)]
                C.mm_group(psC, PSC[0:1, 0:256], pairs, ins=(onescol, osq))
                C.TSi(drow, drow.ap[0:1, 0:256], psC, PSC[0:1, 0:256],
                      1.0 / DV, EPS, ALU.mult, ALU.add)
                C.ACT(drow, drow.ap[0:1, 256:512], drow, drow.ap[0:1, 0:256],
                      AF.Sqrt)
                C.RECIP(drow, drow.ap[0:1, 512:768], drow, drow.ap[0:1, 256:512])
                C.mm_group(psC, PSC[:, 256:512],
                           [(ones_f.ap[0:1, 0:128], drow.ap[0:1, 512:768])],
                           ins=(ones_f, drow))
                C.CPY(rrep_sb, rrep_sb.ap, psC, PSC[:, 256:512])
                for dvc in range(4):
                    C.TT(ont, ont.ap[:, dvc * 256:(dvc + 1) * 256], of32,
                         of32.ap[:, dvc * 256:(dvc + 1) * 256], rrep_sb, rrep_sb.ap)
                    C.TS(ont, ont.ap[:, dvc * 256:(dvc + 1) * 256], ont,
                         ont.ap[:, dvc * 256:(dvc + 1) * 256], gnw_sb,
                         gnw_sb.ap[:, dvc:dvc + 1])
                C.ACT(sil, sil.ap, gsb, gsb.ap[:, h * 1024:(h + 1) * 1024], AF.Silu)
                C.TT(ow_sc, ow_sc.ap[:, h * 1024:(h + 1) * 1024], ont, ont.ap,
                     sil, sil.ap)
            for rc in range(16):
                C.stor(ow_d.ap[rc * 128:(rc + 1) * 128, sc * 256:(sc + 1) * 256],
                       ow_sc, ow_sc.ap[:, rc * 256:(rc + 1) * 256], q="gpsimd",
                       dstb=ow_d)
        A.freebuf("S_sb", "S_bf",
                  "dmask_sb", "crep", "krT_sc", "qs_sc", "ksd_sc", "pt_sc",
                  "of32", "osq", "rrep_sb", "ont", "sil", "ow_sc", "gsb",
                  "kro0", "kro1", "vro0", "vro1", "qrT")

        wo_sb = A.alloc("wo_sb", 16 * D)
        C.load(wo_sb, wo_sb.ap.rearrange("p (c j) -> p c j", c=16),
               IN["Wo"].rearrange("(c p) j -> p c j", p=128), q="gpsimd")
        owt = A.alloc("owt", 2048)
        rout = A.alloc("rout", 1024, F32)
        for tt in range(16):
            C.load(owt, owt.ap.rearrange("p (rc t) -> p rc t", rc=16),
                   ow_d.ap[:, tt * 128:(tt + 1) * 128].rearrange(
                       "(rc p) t -> p rc t", p=128), srcb=ow_d)
            for jt in range(2):
                pairs = [(owt.ap[:, rc * 128:(rc + 1) * 128],
                          wo_sb.ap[:, rc * D + jt * 512:rc * D + jt * 512 + 512])
                         for rc in range(16)]
                C.mm_group(psA[jt], psA[jt].ap, pairs, ins=(owt, wo_sb))
                C.CPY(rout, rout.ap[:, jt * 512:(jt + 1) * 512], psA[jt],
                      psA[jt].ap)
            C.stor(retout_d.ap[tt * 128:(tt + 1) * 128, :], rout, rout.ap,
                   q="gpsimd", dstb=retout_d)
        A.freebuf("wo_sb", "owt", "rout")


    if phase_limit >= 3:
        bo_rep = A.alloc("bo_rep", D, F32)
        l1g_rep = A.alloc("l1g_rep", D, F32)
        l1b_rep = A.alloc("l1b_rep", D, F32)
        rep_row_from_dram(bo_rep, IN["bo_v"], D)
        rep_row_from_dram(l1g_rep, IN["l1g"], D)
        rep_row_from_dram(l1b_rep, IN["l1b"], D)

        qmT = A.alloc("qmT3", 8 * TL)
        for qc in range(8):
            C.load(qmT, qmT.ap[:, qc * TL:(qc + 1) * TL],
                   qmT_d.ap[qc * 128:(qc + 1) * 128, :], srcb=qmT_d)
        c = stc
        C.RED(c, c.ap[0:2, 8:16], maxq_sb,
              maxq_sb.ap.rearrange("r (c4 s) -> r c4 s", s=4))
        C.RED(c, c.ap[0:2, 16:24], maxk_sb,
              maxk_sb.ap.rearrange("r (c8 s) -> r c8 s", s=16))
        C.ACT(c, c.ap[0:2, 8:16], c, c.ap[0:2, 8:16], AF.Sqrt)
        C.ACT(c, c.ap[0:2, 16:24], c, c.ap[0:2, 16:24], AF.Sqrt)
        C.TT(c, c.ap[0:2, 0:8], c, c.ap[0:2, 8:16], c, c.ap[0:2, 16:24])
        C.TSi(c, c.ap[0:2, 0:8], c, c.ap[0:2, 0:8], -1.0)
        C.transp(psC, PSC[0:8, 16:18], c.ap[0:2, 0:8], idf.ap[0:2, 0:2],
                 ins=(stc, idf))
        C.CPY(c, c.ap[0:8, 4:6], psC, PSC[0:8, 16:18])
        C.stor(row_d.ap[0:1, 0:16].rearrange("o (q r) -> (o q) r", r=2),
               c, c.ap[0:8, 4:6], dstb=row_d)
        C.load(c, c.ap[0:1, 0:16], row_d.ap[0:1, 0:16], srcb=row_d)
        C.mm_group(psC, PSC[:, 0:16],
                   [(ones_f.ap[0:1, 0:128], c.ap[0:1, 0:16])], ins=(ones_f, c))
        C.CPY(mh_bias, mh_bias.ap[:, 0:NH], psC, PSC[:, 0:16])
        mv3 = mh_bias.ap[:, 0:NH].rearrange("p (hp two) -> p hp two", two=2)
        C.op("vector", lambda E, O=mh_bias.ap[:, NH:NH + 8],
             A_=mv3[:, :, 0:1].squeeze(2), B_=mv3[:, :, 1:2].squeeze(2):
             E.tensor_tensor(O, A_, B_, ALU.min),
             ins=(mh_bias,), outs=(mh_bias,), inc="ve")

        dacc_a = A.alloc("dacc_a", TL, F32)
        C.memset(dacc_a, dacc_a.ap[0:16, :], 0.0, eng="vector", inc="ve")
        o_acc = A.alloc("o_acc", 8 * TL, F32)
        kTb = [A.alloc("kT0", 4096), A.alloc("kT1", 4096)]
        vtb = [A.alloc("vt0", 4096), A.alloc("vt1", 4096)]
        PtP = [A.alloc("PtP0", 1024), A.alloc("PtP1", 1024)]
        psST = [Buf(PSA[:, i * 512:(i + 1) * 512], f"psST{i}") for i in range(4)]
        psPV = Buf(PSB[:, 0:512], "psPV")
        psDA = Buf(PSC[0:1, 0:512], "psDA")
        psDB = Buf(PSC[64:65, 0:512], "psDB")

        def load_seg(kdst, vdst, seg):
            for kc in range(8):
                C.load(kdst, kdst.ap[:, kc * 512:(kc + 1) * 512],
                       mkT_d.ap[kc * 128:(kc + 1) * 128, seg * 512:(seg + 1) * 512],
                       srcb=mkT_d)
            for ti in range(4):
                C.load(vdst, vdst.ap[:, ti * 1024:(ti + 1) * 1024],
                       mv_d.ap[seg * 512 + ti * 128:seg * 512 + (ti + 1) * 128, :],
                       srcb=mv_d)

        for sp in range(8):
            a_b = [(kTb[0], vtb[0]), (kTb[1], vtb[1])]
            load_seg(a_b[0][0], a_b[0][1], sp * 2)
            load_seg(a_b[1][0], a_b[1][1], sp * 2 + 1)
            for hp in range(NH // 2):
                hA, hB = 2 * hp, 2 * hp + 1
                for qt in range(4):
                    q0 = qt * 512

                    def emit_st(gk):
                        s2, kt = gk // 4, gk % 4
                        kT_, _ = a_b[s2]
                        sA, sB = psST[(gk % 2) * 2], psST[(gk % 2) * 2 + 1]
                        ptp = PtP[gk % 2]
                        for (base, ps_) in ((0, sA), (64, sB)):
                            C.mm_group(ps_, ps_.ap,
                                       [(kT_.ap[base:base + 64,
                                                hp * 512 + kt * 128:
                                                hp * 512 + kt * 128 + 128],
                                         qmT.ap[base:base + 64,
                                                hp * TL + q0:hp * TL + q0 + 512])],
                                       ins=(kT_, qmT))
                        C.op("scalar", lambda E, O=ptp.ap,
                             I=PSA[:, (gk % 2) * 1024:(gk % 2 + 1) * 1024],
                             b=mh_bias.ap[:, NH + hp:NH + hp + 1]:
                             E.activation(O, I, AF.Exp, bias=b),
                             ins=(sA, sB, mh_bias), outs=(ptp,), inc="ac")

                    def emit_pv(gk):
                        s2, kt = gk // 4, gk % 4
                        _, vt_ = a_b[s2]
                        ptp = PtP[gk % 2]
                        pta = Buf(ptp.ap[:, 0:512], "pta"); pta.wr = ptp.wr
                        ptb = Buf(ptp.ap[:, 512:1024], "ptb"); ptb.wr = ptp.wr
                        pta.rd = ptp.rd; ptb.rd = ptp.rd
                        st_, sp_ = (gk == 0), (gk == 7)
                        exPV = ([psPV.wr] + psPV.rd) if st_ else []
                        exD = ([psDA.wr, psDB.wr] + psDA.rd + psDB.rd) \
                            if st_ else []
                        C.op("tensor", lambda E, P=PSB[0:64, 0:512],
                             L=vt_.ap[:, kt * 1024 + hp * 128:
                                      kt * 1024 + hp * 128 + 64],
                             R=pta.ap, a=st_, b=sp_:
                             E.matmul(P, L, R, start=a, stop=b,
                                      tile_position=(0, 0)),
                             ins=(vt_, pta), outs=(), inc="pe", extra=exPV)
                        C.op("tensor", lambda E, P=PSB[64:128, 0:512],
                             L=vt_.ap[:, kt * 1024 + hp * 128 + 64:
                                      kt * 1024 + hp * 128 + 128],
                             R=ptb.ap, a=st_, b=sp_:
                             E.matmul(P, L, R, start=a, stop=b,
                                      tile_position=(0, 64)),
                             ins=(vt_, ptb), outs=(psPV,) if sp_ else (),
                             inc="pe")
                        C.op("tensor", lambda E, P=psDA.ap, L=onescol.ap,
                             R=pta.ap, a=st_, b=sp_:
                             E.matmul(P, L, R, start=a, stop=b),
                             ins=(onescol, pta), outs=(), inc="pe", extra=exD)
                        C.op("tensor", lambda E, P=psDB.ap, L=onescol.ap,
                             R=ptb.ap, a=st_, b=sp_:
                             E.matmul(P, L, R, start=a, stop=b,
                                      tile_position=(0, 64)),
                             ins=(onescol, ptb), outs=(psDB,) if sp_ else (),
                             inc="pe")

                    emit_st(0)
                    for gk in range(1, 8):
                        emit_st(gk)
                        emit_pv(gk - 1)
                    emit_pv(7)
                    dst = o_acc.ap[:, hp * 2048 + q0:hp * 2048 + q0 + 512]
                    if sp == 0:
                        C.op("vector", lambda E, O=dst, I=PSB[:, 0:512]:
                             E.tensor_copy(O, I), ins=(psPV,), outs=(o_acc,),
                             inc="ve")
                    else:
                        C.op("vector", lambda E, O=dst, I=PSB[:, 0:512], X=dst:
                             E.tensor_tensor(O, I, X, ALU.add),
                             ins=(psPV,), outs=(o_acc,), inc="ve")
                    C.op("scalar", lambda E, O=dstage.ap[0:1, q0:q0 + 512],
                         I=psDA.ap: E.activation(O, I, AF.Identity),
                         ins=(psDA, psDB), outs=(dstage,), inc="ac")
                    C.op("scalar", lambda E, O=dstage2.ap[0:1, q0:q0 + 512],
                         I=psDB.ap: E.activation(O, I, AF.Identity),
                         ins=(psDA, psDB), outs=(dstage2,), inc="ac")
                C.stor(dacc_a.ap[hA:hA + 1, :], dstage, dstage.ap[0:1, :],
                       q="gpsimd", dstb=dacc_a, accum=True)
                C.stor(dacc_a.ap[hB:hB + 1, :], dstage2, dstage2.ap[0:1, :],
                       q="gpsimd", dstb=dacc_a, accum=True)
        A.freebuf("kT0", "kT1", "vt0", "vt1", "PtP0", "PtP1")
        C.RECIP(dacc_a, dacc_a.ap[0:16, :], dacc_a, dacc_a.ap[0:16, :])
        on_bf = A.alloc("on_bf", 8 * TL)
        rrep3 = A.alloc("rrep3", TL, F32)
        for h in range(NH):
            hc, hr = h // 2, h % 2
            C.load(drow, drow.ap[0:1, :], dacc_a.ap[h:h + 1, :], srcb=dacc_a)
            for j4 in range(4):
                C.mm_group(psC, PSC[:, 0:512],
                           [(ones_f.ap[0:1, 0:128],
                             drow.ap[0:1, j4 * 512:(j4 + 1) * 512])],
                           ins=(ones_f, drow))
                C.CPY(rrep3, rrep3.ap[:, j4 * 512:(j4 + 1) * 512], psC, PSC[:, 0:512])
            C.op("vector", lambda E,
                 O=on_bf.ap[hr * 64:hr * 64 + 64, hc * 2048:(hc + 1) * 2048],
                 I=o_acc.ap[hr * 64:hr * 64 + 64, hc * 2048:(hc + 1) * 2048],
                 X=rrep3.ap[hr * 64:hr * 64 + 64, :]:
                 E.tensor_tensor(O, I, X, ALU.mult),
                 ins=(o_acc, rrep3), outs=(on_bf,), inc="ve")
        A.freebuf("o_acc", "dacc_a", "rrep3", "qmT3")

        woT_sb = A.alloc("woT_sb", 8 * D)
        C.load(woT_sb, woT_sb.ap.rearrange("p (c j) -> p c j", c=8),
               rearr_w(IN["woT"]), q="gpsimd")
        h1T = A.alloc("h1T", 8 * TL)
        xot2 = [A.alloc("xot", 1024, F32), A.alloc("xotb", 1024, F32)]
        uf2 = [A.alloc("uf", 1024, F32), A.alloc("ufb", 1024, F32)]
        xnt2 = [A.alloc("xnt", 1024, F32), A.alloc("xntb", 1024, F32)]
        tf2 = [A.alloc("tf", 1024, F32), A.alloc("tfb", 1024, F32)]
        sq32 = [A.alloc("sq3", 1024), A.alloc("sq3b", 1024)]
        h1n2 = [A.alloc("h1n", 1024, F32), A.alloc("h1nb", 1024, F32)]
        h1b32 = [A.alloc("h1b3", 1024), A.alloc("h1b3b", 1024)]
        cL = [Buf(stc.ap[:, 16:24], "cL0"), Buf(stc.ap[:, 24:32], "cL1")]
        for tt in range(16):
            p = tt % 2
            xot, uf, xnt, tf = xot2[p], uf2[p], xnt2[p], tf2[p]
            sq3, h1n, h1b3, cp = sq32[p], h1n2[p], h1b32[p], cL[p]
            pj = (psA[0], psA[1]) if p == 0 else (psA[2], psA[3])
            pslc = PSA[:, 0:1024] if p == 0 else PSA[:, 1024:2048]
            C.load(xot, xot.ap, IN["x_own"][tt * 128:(tt + 1) * 128, :])
            for j in range(2):
                pairs = [(on_bf.ap[:, rc * 2048 + tt * 128:rc * 2048 + tt * 128 + 128],
                          woT_sb.ap[:, rc * D + j * 512:rc * D + j * 512 + 512])
                         for rc in range(8)]
                C.mm_group(pj[j], pj[j].ap, pairs, ins=(on_bf, woT_sb))
            C.op("vector", lambda E, O=uf.ap, I=pslc, X=bo_rep.ap:
                 E.tensor_tensor(O, I, X, ALU.add),
                 ins=(pj[0], pj[1], bo_rep), outs=(uf,), inc="ve")
            C.TS(xnt, xnt.ap, xot, xot.ap, mu1o, mu1o.ap[:, tt:tt + 1],
                 rs1o, rs1o.ap[:, tt:tt + 1], ALU.subtract, ALU.mult)
            C.STT(tf, tf.ap, uf, uf.ap, 1.0, xnt, xnt.ap, ALU.mult, ALU.add,
                  accb=cp, acc=cp.ap[:, 0:1])
            C.ACT(sq3, sq3.ap, tf, tf.ap, AF.Square, accb=cp, acc=cp.ap[:, 1:2])
            C.TSi(cp, cp.ap[:, 2:3], cp, cp.ap[:, 0:1], 1.0 / D)
            C.TT(cp, cp.ap[:, 3:4], cp, cp.ap[:, 2:3], cp, cp.ap[:, 2:3])
            C.TSi(cp, cp.ap[:, 4:5], cp, cp.ap[:, 1:2], 1.0 / D, EPS,
                  ALU.mult, ALU.add)
            C.TT(cp, cp.ap[:, 5:6], cp, cp.ap[:, 4:5], cp, cp.ap[:, 3:4],
                 ALU.subtract)
            C.ACT(cp, cp.ap[:, 6:7], cp, cp.ap[:, 5:6], AF.Sqrt)
            C.RECIP(cp, cp.ap[:, 7:8], cp, cp.ap[:, 6:7])
            C.TS(h1n, h1n.ap, tf, tf.ap, cp, cp.ap[:, 2:3], cp, cp.ap[:, 7:8],
                 ALU.subtract, ALU.mult)
            C.TT(h1n, h1n.ap, h1n, h1n.ap, l1g_rep, l1g_rep.ap)
            C.TT(h1n, h1n.ap, h1n, h1n.ap, l1b_rep, l1b_rep.ap, ALU.add)
            C.stor(h1_d.ap[tt * 128:(tt + 1) * 128, :], h1n, h1n.ap, dstb=h1_d)
            C.CPY(h1b3, h1b3.ap, h1n, h1n.ap)
            transpose_into(h1b3, h1T, tt, colsz=TL)
        A.freebuf("on_bf", "woT_sb", "xot", "uf", "xnt", "tf", "sq3", "h1n",
                  "h1b3", "xotb", "ufb", "xntb", "tfb", "sq3b", "h1nb",
                  "h1b3b", "bo_rep", "l1g_rep", "l1b_rep")


    if phase_limit >= 4:
        l2g_rep = A.alloc("l2g_rep", D, F32)
        l2b_rep = A.alloc("l2b_rep", D, F32)
        rep_row_from_dram(l2g_rep, IN["l2g"], D)
        rep_row_from_dram(l2b_rep, IN["l2b"], D)
        c = stc
        gl4 = A.alloc("gl4", TL, F32)
        el8 = A.alloc("el8", TL, F32)
        for ts4 in range(4):
            pairs = [(wgrp_sb.ap[:, cc * 4:(cc + 1) * 4],
                      h1T.ap[:, cc * TL + ts4 * 512:cc * TL + ts4 * 512 + 512])
                     for cc in range(8)]
            C.mm_group(psC, PSC[0:4, 0:512], pairs, ins=(wgrp_sb, h1T))
            C.op("scalar", lambda E, O=gl4.ap[0:4, ts4 * 512:(ts4 + 1) * 512],
                 I=PSC[0:4, 0:512], b=bgrp_sb.ap:
                 E.activation(O, I, AF.Identity, bias=b),
                 ins=(psC, bgrp_sb), outs=(gl4,), inc="ac")
            pairs = [(wexp_sb.ap[:, cc * 8:(cc + 1) * 8],
                      h1T.ap[:, cc * TL + ts4 * 512:cc * TL + ts4 * 512 + 512])
                     for cc in range(8)]
            C.mm_group(psC, PSC[0:8, 0:512], pairs, ins=(wexp_sb, h1T))
            C.op("scalar", lambda E, O=el8.ap[0:8, ts4 * 512:(ts4 + 1) * 512],
                 I=PSC[0:8, 0:512], b=bexp_sb.ap:
                 E.activation(O, I, AF.Identity, bias=b),
                 ins=(psC, bexp_sb), outs=(el8,), inc="ac")
        gtok = A.alloc("gtok", 64, F32)
        etok = A.alloc("etok", 128, F32)
        for tt in range(16):
            C.transp(psC, PSC[0:128, 0:4], gl4.ap[0:4, tt * 128:(tt + 1) * 128],
                     idf.ap[0:4, 0:4], ins=(gl4, idf))
            C.CPY(gtok, gtok.ap[:, tt * 4:(tt + 1) * 4], psC, PSC[0:128, 0:4])
            C.transp(psC, PSC[0:128, 4:12], el8.ap[0:8, tt * 128:(tt + 1) * 128],
                     idf.ap[0:8, 0:8], ins=(el8, idf))
            C.CPY(etok, etok.ap[:, tt * 8:(tt + 1) * 8], psC, PSC[0:128, 4:12])
        A.freebuf("gl4", "el8")
        comb8 = A.alloc("comb8", 128, F32)
        r4 = A.alloc("r4", 8, F32)
        for tt in range(16):
            g4 = gtok.ap[:, tt * 4:(tt + 1) * 4]
            C.RED(c, c.ap[:, 0:1], gtok, g4, alu=ALU.max, negate=True)
            C.op("scalar", lambda E, O=r4.ap[:, 0:4], I=g4,
                 b=c.ap[:, 0:1]: E.activation(O, I, AF.Exp, bias=b),
                 ins=(gtok, stc), outs=(r4,), inc="ac")
            C.RED(c, c.ap[:, 1:2], r4, r4.ap[:, 0:4], alu=ALU.add)
            C.RED(c, c.ap[:, 2:3], r4, r4.ap[:, 0:4], alu=ALU.max)
            C.RECIP(c, c.ap[:, 3:4], c, c.ap[:, 1:2])
            C.TT(c, c.ap[:, 4:5], c, c.ap[:, 2:3], c, c.ap[:, 3:4])
            C.TS(r4, r4.ap[:, 4:8], r4, r4.ap[:, 0:4],
                 c, c.ap[:, 2:3], None, None, ALU.is_equal)
            ev = etok.ap.rearrange("p (t g) -> p t g", g=8)[:, tt, :].rearrange(
                "p (g2 two) -> p g2 two", two=2)
            C.op("vector", lambda E, O=c.ap[:, 8:12], A_=ev[:, :, 0].squeeze() if
                 False else ev[:, :, 0:1].squeeze(2),
                 B_=ev[:, :, 1:2].squeeze(2):
                 E.tensor_tensor(O, A_, B_, ALU.subtract),
                 ins=(etok,), outs=(stc,), inc="ve")
            C.TSi(c, c.ap[:, 12:16], c, c.ap[:, 8:12], 0.0, None, ALU.is_lt)
            C.op("scalar", lambda E, O=c.ap[:, 16:20], I=c.ap[:, 8:12]:
                 E.activation(O, I, AF.Abs), ins=(stc,), outs=(stc,), inc="ac")
            C.op("scalar", lambda E, O=c.ap[:, 16:20], I=c.ap[:, 16:20]:
                 E.activation(O, I, AF.Sigmoid), ins=(stc,), outs=(stc,), inc="ac")
            C.TT(c, c.ap[:, 20:24], stc, c.ap[:, 16:20], r4, r4.ap[:, 4:8])
            C.RED(c, c.ap[:, 5:6], stc, c.ap[:, 20:24], alu=ALU.add)
            C.TT(c, c.ap[:, 6:7], c, c.ap[:, 4:5], c, c.ap[:, 5:6])
            cm8 = comb8.ap.rearrange("p (t g) -> p t g", g=8)[:, tt, :]
            cm2 = cm8.rearrange("p (g2 two) -> p g2 two", two=2)
            C.op("vector", lambda E, O=cm2[:, :, 1:2].squeeze(2), A_=r4.ap[:, 4:8],
                 B_=c.ap[:, 12:16]: E.tensor_tensor(O, A_, B_, ALU.mult),
                 ins=(r4, stc), outs=(comb8,), inc="ve")
            C.op("vector", lambda E, O=cm2[:, :, 0:1].squeeze(2), A_=r4.ap[:, 4:8],
                 B_=cm2[:, :, 1:2].squeeze(2):
                 E.tensor_tensor(O, A_, B_, ALU.subtract),
                 ins=(r4, comb8), outs=(comb8,), inc="ve")
            C.TS(comb8, cm8, comb8, cm8, c, c.ap[:, 6:7], None, None, ALU.mult)
        combT = A.alloc("combT", TL)
        for tt in range(16):
            C.transp(psC, PSC[0:8, 0:128],
                     comb8.ap.rearrange("p (t g) -> p t g", g=8)[:, tt, :],
                     idf.ap, ins=(comb8, idf))
            C.CPY(combT, combT.ap[0:8, tt * 128:(tt + 1) * 128], psC, PSC[0:8, 0:128])
        A.freebuf("gtok", "etok", "r4", "comb8")
        crepb = A.alloc("crepb", TL)

        w1q = [A.alloc("w1q0", 8192), A.alloc("w1q1", 8192)]
        w2q = [A.alloc("w2q0", 8192), A.alloc("w2q1", 8192)]
        hbt = A.alloc("hbt", 8 * TL)
        yst = [A.alloc("yst0", 1024, F32), A.alloc("yst1", 1024, F32)]
        y_tiles = [Buf(y_d.ap[tt * 128:(tt + 1) * 128, :], f"y{tt}")
                   for tt in range(16)]
        for e_ in range(E):
            for j4 in range(4):
                C.mm_group(psC, PSC[:, 0:512],
                           [(sel8_sb.ap[:, e_ * 128:(e_ + 1) * 128],
                             combT.ap[0:8, j4 * 512:(j4 + 1) * 512])],
                           ins=(sel8_sb, combT))
                C.CPY(crepb, crepb.ap[:, j4 * 512:(j4 + 1) * 512], psC, PSC[:, 0:512])
            for hf in range(2):
                idx = e_ * 2 + hf
                wb = w1q[idx % 2]
                if e_ == 0 and hf == 0:
                    C.load(wb, wb.ap.rearrange("p (cc j) -> p cc j", cc=8),
                           IN["w1"][e_].rearrange("(cc p) j -> p cc j", p=128)
                           [:, :, hf * 1024:(hf + 1) * 1024], q="gpsimd")
                wb2 = w2q[idx % 2]
                C.load(wb2, wb2.ap.rearrange("p (cc j) -> p cc j", cc=8),
                       IN["w2"][e_][hf * 1024:(hf + 1) * 1024, :].rearrange(
                           "(cc p) j -> p cc j", p=128), q="gpsimd")
                for tg in range(4):
                    tok0 = tg * 512
                    for fc in range(8):
                        pairs = [(wb.ap[:, cc * 1024 + fc * 128:
                                        cc * 1024 + fc * 128 + 128],
                                  h1T.ap[:, cc * TL + tok0:cc * TL + tok0 + 512])
                                 for cc in range(8)]
                        C.mm_group(psB[fc % 2],
                                   PSB[:, (fc % 2) * 512:(fc % 2 + 1) * 512],
                                   pairs, ins=(wb, h1T))
                        fci = hf * 8 + fc
                        C.op("scalar", lambda E,
                             O=hbt.ap[:, fc * TL + tok0:fc * TL + tok0 + 512],
                             I=PSB[:, (fc % 2) * 512:(fc % 2 + 1) * 512],
                             b=b1c.ap[:, e_ * 16 + fci:e_ * 16 + fci + 1]:
                             E.activation(O, I, AF.Relu, bias=b),
                             ins=(psB[fc % 2], b1c), outs=(hbt,), inc="ac")
                        C.op("vector", lambda E,
                             O=hbt.ap[:, fc * TL + tok0:fc * TL + tok0 + 512],
                             A_=hbt.ap[:, fc * TL + tok0:fc * TL + tok0 + 512],
                             B_=crepb.ap[:, tok0:tok0 + 512]:
                             E.tensor_tensor(O, A_, B_, ALU.mult),
                             ins=(hbt, crepb), outs=(hbt,), inc="ve")
                nidx = idx + 1
                if nidx < 16:
                    ne, nhf = nidx // 2, nidx % 2
                    nwb = w1q[nidx % 2]
                    C.load(nwb, nwb.ap.rearrange("p (cc j) -> p cc j", cc=8),
                           IN["w1"][ne].rearrange("(cc p) j -> p cc j", p=128)
                           [:, :, nhf * 1024:(nhf + 1) * 1024], q="gpsimd")
                for t8 in range(8):
                    tok0 = t8 * 256
                    for t1 in range(2):
                        tt = t8 * 2 + t1
                        for dj in range(2):
                            slot = psA[t1 * 2 + dj]
                            after = [wb2.wr, hbt.wr, combT.wr, b2_sb.wr,
                                     slot.wr] + slot.rd
                            tok = None
                            if e_ == 0 and hf == 0:
                                tok = S.emit(
                                    "tensor",
                                    lambda E, P=slot.ap,
                                    L=combT.ap[0:8, tt * 128:tt * 128 + 128],
                                    R=b2_sb.ap[0:8, dj * 512:(dj + 1) * 512]:
                                    E.matmul(P, L, R, start=True, stop=False),
                                    inc="pe", after=after)
                            for fc in range(8):
                                tok = S.emit(
                                    "tensor",
                                    lambda E, P=slot.ap,
                                    L=hbt.ap[:, fc * TL + tok0 + t1 * 128:
                                             fc * TL + tok0 + t1 * 128 + 128],
                                    R=wb2.ap[:, fc * 1024 + dj * 512:
                                             fc * 1024 + dj * 512 + 512],
                                    a=(fc == 0 and not (e_ == 0 and hf == 0)),
                                    b=(fc == 7):
                                    E.matmul(P, L, R, start=a, stop=b),
                                    inc="pe",
                                    after=after if fc == 0 else ())
                            for b_ in (wb2, hbt, combT, b2_sb):
                                b_.rd.append(tok)
                            slot.wr = tok
                            slot.rd = []
                        ys = yst[tt % 2]
                        C.op("vector", lambda E, O=ys.ap,
                             I=PSA[:, (t1 * 2) * 512:(t1 * 2 + 2) * 512]:
                             E.tensor_copy(O, I),
                             ins=(psA[t1 * 2], psA[t1 * 2 + 1]), outs=(ys,),
                             inc="ve")
                        C.stor(y_tiles[tt].ap, ys, ys.ap,
                               q="gpsimd", dstb=y_tiles[tt],
                               accum=not (e_ == 0 and hf == 0))
        A.freebuf("w1q0", "w1q1", "w2q0", "w2q1", "hbt", "crepb", "combT",
                  "h1T", "yst0", "yst1")

        yac2 = [A.alloc("yac", 1024, F32), A.alloc("yacb", 1024, F32)]
        ret42 = [A.alloc("ret4", 1024, F32), A.alloc("ret4b", 1024, F32)]
        xo42 = [A.alloc("xo4", 1024, F32), A.alloc("xo4b", 1024, F32)]
        h142 = [A.alloc("h14", 1024, F32), A.alloc("h14b", 1024, F32)]
        sq42 = [A.alloc("sq4", 1024), A.alloc("sq4b", 1024)]
        ffn42 = [A.alloc("ffn4", 1024, F32), A.alloc("ffn4b", 1024, F32)]
        ot42 = [A.alloc("ot4", 1024, F32), A.alloc("ot4b", 1024, F32)]
        for tt in range(16):
            p = tt % 2
            yac, ret4, xo4, h14 = yac2[p], ret42[p], xo42[p], h142[p]
            sq4, ffn4, ot4, cp = sq42[p], ffn42[p], ot42[p], cL[p]
            C.load(ret4, ret4.ap, retout_d.ap[tt * 128:(tt + 1) * 128, :],
                   srcb=retout_d)
            C.load(xo4, xo4.ap, IN["x_own"][tt * 128:(tt + 1) * 128, :])
            C.load(h14, h14.ap, h1_d.ap[tt * 128:(tt + 1) * 128, :], srcb=h1_d)
            C.load(yac, yac.ap, y_tiles[tt].ap, srcb=y_tiles[tt])
            C.TT(yac, yac.ap, yac, yac.ap, h14, h14.ap, ALU.add)
            C.RED(cp, cp.ap[:, 0:1], yac, yac.ap, alu=ALU.add)
            C.ACT(sq4, sq4.ap, yac, yac.ap, AF.Square, accb=cp,
                  acc=cp.ap[:, 1:2])
            C.TSi(cp, cp.ap[:, 2:3], cp, cp.ap[:, 0:1], 1.0 / D)
            C.TT(cp, cp.ap[:, 3:4], cp, cp.ap[:, 2:3], cp, cp.ap[:, 2:3])
            C.TSi(cp, cp.ap[:, 4:5], cp, cp.ap[:, 1:2], 1.0 / D, EPS,
                  ALU.mult, ALU.add)
            C.TT(cp, cp.ap[:, 5:6], cp, cp.ap[:, 4:5], cp, cp.ap[:, 3:4],
                 ALU.subtract)
            C.ACT(cp, cp.ap[:, 6:7], cp, cp.ap[:, 5:6], AF.Sqrt)
            C.RECIP(cp, cp.ap[:, 7:8], cp, cp.ap[:, 6:7])
            C.TS(ffn4, ffn4.ap, yac, yac.ap, cp, cp.ap[:, 2:3],
                 cp, cp.ap[:, 7:8], ALU.subtract, ALU.mult)
            C.TT(ffn4, ffn4.ap, ffn4, ffn4.ap, l2g_rep, l2g_rep.ap)
            C.TT(ffn4, ffn4.ap, ffn4, ffn4.ap, l2b_rep, l2b_rep.ap, ALU.add)
            C.TS(xo4, xo4.ap, xo4, xo4.ap, mu1o, mu1o.ap[:, tt:tt + 1],
                 rs1o, rs1o.ap[:, tt:tt + 1], ALU.subtract, ALU.mult)
            C.load(ot4, ot4.ap, IN["x_own"][tt * 128:(tt + 1) * 128, :])
            C.TT(ot4, ot4.ap, ot4, ot4.ap, xo4, xo4.ap, ALU.add)
            C.TT(ot4, ot4.ap, ot4, ot4.ap, ret4, ret4.ap, ALU.add)
            C.TT(ot4, ot4.ap, ot4, ot4.ap, ffn4, ffn4.ap, ALU.add)
            C.stor(out_d[tt * 128:(tt + 1) * 128, :], ot4, ot4.ap, dstb=out_b)
        A.freebuf("yac", "ret4", "xo4", "h14", "sq4", "ffn4", "ot4",
                  "yacb", "ret4b", "xo4b", "h14b", "sq4b", "ffn4b", "ot4b")



    if phase_limit < 4:
        fin = A.alloc("fin", 1024, F32)
        for i in range(16):
            C.load(fin, fin.ap, IN["x_own"][i * 128:(i + 1) * 128, :])
            C.stor(out_d[i * 128:(i + 1) * 128, :], fin, fin.ap, dstb=out_b)
    S.barrier()

    with nc.Block() as blk:
        S.replay(blk, sems)
    es.close()
    return nc


def host_tables(core):
    s = core % 4
    i = np.arange(0, DK, 2, dtype=np.float32) / DK
    inv_freq = (10000.0 ** -i).astype(np.float32)
    t_idx = np.arange(T, dtype=np.float32)
    fr = t_idx[:, None] * inv_freq[None, :]
    cos = np.cos(fr).astype(NPBF)
    sin = np.sin(fr).astype(NPBF)
    own = slice(s * TL, (s + 1) * TL)
    g = np.float32(GAMMA)
    ii = np.arange(CP, dtype=np.float32)
    dm = np.where(ii[:, None] >= ii[None, :],
                  g[:, None, None] ** (ii[:, None] - ii[None, :]), 0.0
                  ).astype(np.float32)
    dmT = dm.transpose(0, 2, 1)
    dmaskT = np.ascontiguousarray(
        dmT.reshape(H, 2, 128, CP)).reshape(H * 2 * 128, CP)
    crossrep = np.broadcast_to((g[:, None] ** (ii + 1.0)).astype(NPBF)
                               .reshape(1, H * CP), (128, H * CP))
    kdec = np.zeros((128, H * 2), np.float32)
    for h in range(H):
        for half in range(2):
            p = np.arange(128, dtype=np.float32) + half * 128
            kdec[:, h * 2 + half] = g[h] ** (CP - 1.0 - p)
    pfx = np.zeros((128, NPFX * H * 2), np.float32)
    for m in range(NPFX):
        for h in range(H):
            for half in range(2):
                p = np.arange(128, dtype=np.float32) + half * 128
                if m < 8 * s:
                    w = g[h] ** ((CP - 1.0 - p) + CP * (8 * s - 1 - m))
                else:
                    w = np.zeros(128, np.float32)
                pfx[:, (m * H + h) * 2 + half] = w
    sel8 = np.zeros((8, 8 * 128), NPBF)
    for e in range(8):
        sel8[e, e * 128:(e + 1) * 128] = 1.0
    pairsel_np = np.zeros((128, 2), NPBF)
    pairsel_np[0:64, 0] = 1.0
    pairsel_np[64:128, 1] = 1.0
    return {
        "cos_tok": np.ascontiguousarray(cos),
        "sin_tok": np.ascontiguousarray(sin),
        "cos_own": np.ascontiguousarray(cos[own]),
        "sin_own": np.ascontiguousarray(sin[own]),
        "cos_f": np.ascontiguousarray(cos[own].T),
        "sin_f": np.ascontiguousarray(sin[own].T),
        "dmaskT": dmaskT,
        "crossrep": np.ascontiguousarray(crossrep),
        "kdec_c": kdec,
        "pfx_c": pfx,
        "ident_b": np.eye(128, dtype=NPBF),
        "ident_f": np.eye(128, dtype=np.float32),
        "ones_row_b": np.ones((1, 512), NPBF),
        "ones_row_f": np.ones((1, 128), np.float32),
        "ones_col_b": np.ones((128, 1), NPBF),
        "pairsel": pairsel_np,
        "selrow8": sel8,
        "zeros_col": np.zeros((128, 1), np.float32),
    }


def make_inmaps(inputs):
    f32 = lambda k: np.ascontiguousarray(np.asarray(inputs[k], np.float32))
    x = f32("L0_attn_x")
    base = {
        "Wq": f32("L0_attn_Wq"), "Wk": f32("L0_attn_Wk"),
        "Wv": f32("L0_attn_Wv"), "Wg": f32("L0_attn_Wg"),
        "Wo": f32("L0_attn_Wo"),
        "wiT": np.ascontiguousarray(f32("L0_ffn_in_proj_w").T),
        "woT": np.ascontiguousarray(f32("L0_ffn_out_proj_w").T),
        "w1": f32("L0_ffn_w1"), "w2": f32("L0_ffn_w2"),
        "wgrp": f32("L0_ffn_w_group"),
        "wexp": np.ascontiguousarray(
            f32("L0_ffn_w_expert").transpose(1, 0, 2).reshape(D, E)),
        "ln_w": f32("L0_attn_ln_w").reshape(1, D),
        "ln_b": f32("L0_attn_ln_b").reshape(1, D),
        "gnw": f32("L0_attn_g_norm_w").reshape(1, DV),
        "bi": f32("L0_ffn_in_proj_b").reshape(1, 3 * D),
        "bo_v": f32("L0_ffn_out_proj_b").reshape(1, D),
        "l1g": f32("L0_ffn_ln1_g").reshape(1, D),
        "l1b": f32("L0_ffn_ln1_b").reshape(1, D),
        "l2g": f32("L0_ffn_ln2_g").reshape(1, D),
        "l2b": f32("L0_ffn_ln2_b").reshape(1, D),
        "bgrp": f32("L0_ffn_b_group").reshape(4, 1),
        "bexp": f32("L0_ffn_b_expert").reshape(E, 1),
        "b1": f32("L0_ffn_b1"), "b2": f32("L0_ffn_b2"),
    }
    maps = []
    for core in range(8):
        b, s = core // 4, core % 4
        m = dict(base)
        m["xb"] = np.ascontiguousarray(x[b])
        m["x_own"] = np.ascontiguousarray(x[b, s * TL:(s + 1) * TL])
        m.update(host_tables(core))
        maps.append(m)
    return maps


def kernel(**inputs):
    maps = make_inmaps(inputs)
    nc = build_nc()
    res = run_bass_kernel_spmd(nc, maps, core_ids=list(range(8)))
    parts = [res.results[c]["out"] for c in range(8)]
    outp = np.empty((B, T, D), np.float32)
    for core in range(8):
        b, s = core // 4, core % 4
        outp[b, s * TL:(s + 1) * TL] = parts[core]
    return outp
